# version 05
# speedup vs baseline: 1.0654x; 1.0240x over previous
import numpy as np
from contextlib import ExitStack

import concourse.bass as bass
import concourse.mybir as mybir
from concourse.bass_utils import run_bass_kernel_spmd

F32 = mybir.dt.float32
BF16 = mybir.dt.bfloat16
AF = mybir.ActivationFunctionType
ALU = mybir.AluOpType

D = 2048
SQ = 2048
NB = 8
NDC = D // 128
H = 16
HD = 128
CH = 128
NCH = SQ // CH
DF = 8192
NJC = DF // 128
LN_EPS = 1e-5
ATT_EPS = 1e-6
STN = 16
DTR = 64
NE = 8


def prep_host(inputs):
    f32 = np.float32
    g = {}
    x = np.asarray(inputs["L0_attn_x"], f32)
    g["xT"] = np.ascontiguousarray(x.transpose(0, 2, 1))

    g["wq"] = np.ascontiguousarray(np.asarray(inputs["L0_attn_Wq"], f32).reshape(NDC, 128, D))
    g["wk"] = np.ascontiguousarray(np.asarray(inputs["L0_attn_Wk"], f32).reshape(NDC, 128, D))
    g["wv"] = np.ascontiguousarray(np.asarray(inputs["L0_attn_Wv"], f32).reshape(NDC, 128, D))
    g["wo"] = np.ascontiguousarray(np.asarray(inputs["L0_attn_Wo"], f32).reshape(NDC, 128, D))
    g["bq"] = np.ascontiguousarray(np.asarray(inputs["L0_attn_bq"], f32).reshape(NDC, 128).T)
    g["bk"] = np.ascontiguousarray(np.asarray(inputs["L0_attn_bk"], f32).reshape(NDC, 128).T)
    g["bo"] = np.ascontiguousarray(np.asarray(inputs["L0_attn_bo"], f32).reshape(NDC, 128).T)
    g["bv_row"] = np.ascontiguousarray(np.asarray(inputs["L0_attn_bv"], f32).reshape(1, D))

    W0 = np.asarray(inputs["L0_ffn_W0"], f32)
    W1 = np.asarray(inputs["L0_ffn_W1"], f32)
    g["w0cube"] = np.ascontiguousarray(W0.reshape(NJC, 128, NDC, 128).transpose(0, 3, 2, 1))
    g["w1cube"] = np.ascontiguousarray(W1.reshape(D, NJC, 128).transpose(1, 2, 0))
    g["b0cols"] = np.ascontiguousarray(np.asarray(inputs["L0_ffn_bias0"], f32).reshape(NJC, 128).T)
    g["b1cols"] = np.ascontiguousarray(np.asarray(inputs["L0_ffn_bias1"], f32).reshape(NDC, 128).T)

    g["wdtin"] = np.ascontiguousarray(np.asarray(inputs["L1_attn_W_dt_in"], f32).reshape(DTR, NDC, 128).transpose(2, 1, 0))
    g["wdtout"] = np.ascontiguousarray(np.asarray(inputs["L1_attn_W_dt_out"], f32).T)
    g["wB"] = np.ascontiguousarray(np.asarray(inputs["L1_attn_W_B"], f32).reshape(STN, NDC, 128).transpose(2, 1, 0))
    g["wC"] = np.ascontiguousarray(np.asarray(inputs["L1_attn_W_C"], f32).reshape(STN, NDC, 128).transpose(2, 1, 0))
    g["acols"] = np.ascontiguousarray(np.asarray(inputs["L1_attn_A_log"], f32).reshape(NDC, 128, STN).transpose(1, 0, 2))
    g["dcols"] = np.ascontiguousarray(np.asarray(inputs["L1_attn_D"], f32).reshape(NDC, 128).T)
    g["bdtcols"] = np.ascontiguousarray(np.asarray(inputs["L1_attn_b_dt_out"], f32).reshape(NDC, 128).T)

    ew = np.asarray(inputs["L1_ffn_expert_w"], f32)
    g["wecube"] = np.ascontiguousarray(ew.reshape(NE, NDC, 128, NDC, 128).transpose(0, 3, 2, 1, 4))
    g["gcube"] = np.ascontiguousarray(np.asarray(inputs["L1_ffn_gate_w"], f32).reshape(NDC, 128, NE).transpose(1, 0, 2))
    g["gb_row"] = np.ascontiguousarray(np.asarray(inputs["L1_ffn_gate_b"], f32).reshape(1, NE))
    g["ebcols"] = np.ascontiguousarray(np.asarray(inputs["L1_ffn_expert_b"], f32).reshape(NE, NDC, 128).transpose(2, 0, 1))

    g["c_ones_row"] = np.ones((1, 128), f32)
    g["c_ones_col"] = np.ones((128, 1), f32)
    g["c_tri"] = np.triu(np.ones((128, 128), f32))
    g["c_id"] = np.eye(128, dtype=f32)
    oh8 = np.zeros((NE, NE * 128), f32)
    for i in range(NE):
        oh8[i, i * 128:(i + 1) * 128] = 1.0
    g["c_oh8"] = oh8
    oh16 = np.zeros((STN, STN * 128), f32)
    for i in range(STN):
        oh16[i, i * 128:(i + 1) * 128] = 1.0
    g["c_oh16"] = oh16
    g["c_misc"] = np.tile(np.array([MISC_VALS], f32), (128, 1))
    return g


MISC_VALS = [LN_EPS, -0.5, 1.0, -1.0, ATT_EPS, 0.5, 2.0, 3.0]


def _reg_misc(nc, cmisc):
    t = getattr(cmisc, "tensor", cmisc)
    if getattr(t, "name", "") == "ARENA":
        return
    for i, v in enumerate(MISC_VALS):
        nc.const_aps.register(F32, v, cmisc[:, i:i + 1])


def build_l0f():
    nc = bass.Bass("TRN2", target_bir_lowering=False)
    xin = nc.dram_tensor("xin", [D, SQ], F32, kind="ExternalInput").ap()
    w0c = nc.dram_tensor("w0cube", [NJC, 128, NDC, 128], F32, kind="ExternalInput").ap()
    w1c = nc.dram_tensor("w1cube", [NJC, 128, D], F32, kind="ExternalInput").ap()
    b0c = nc.dram_tensor("b0cols", [128, NJC], F32, kind="ExternalInput").ap()
    b1c = nc.dram_tensor("b1cols", [128, NDC], F32, kind="ExternalInput").ap()
    onesr = nc.dram_tensor("c_ones_row", [1, 128], F32, kind="ExternalInput").ap()
    onesc = nc.dram_tensor("c_ones_col", [128, 1], F32, kind="ExternalInput").ap()
    cmiscd = nc.dram_tensor("c_misc", [128, len(MISC_VALS)], F32, kind="ExternalInput").ap()
    xout = nc.dram_tensor("xout", [D, SQ], F32, kind="ExternalOutput").ap()

    NJG = 8
    NG = NJC // NJG
    SUP = 1024
    NSB = SUP // 512

    with ExitStack() as st:
        e = st.enter_context
        xbf = e(nc.sbuf_tensor("xbf", [128, NDC, SQ], BF16))
        eacc = e(nc.sbuf_tensor("eacc", [128, NDC, SUP], F32))
        w0t = [e(nc.sbuf_tensor(f"w0t{i}", [128, NDC, 128], BF16)) for i in range(2)]
        w1g = [e(nc.sbuf_tensor(f"w1g{i}", [128, D], BF16)) for i in range(NJG)]
        d1 = e(nc.sbuf_tensor("d1", [128, NJG, SUP], BF16))
        sqt = [e(nc.sbuf_tensor(f"sqt{i}", [128, 512], F32)) for i in range(2)]
        x2t = [e(nc.sbuf_tensor(f"x2t{i}", [128, 512], F32)) for i in range(2)]
        tnrm = e(nc.sbuf_tensor("tnrm", [128, 512], F32))
        b0s = e(nc.sbuf_tensor("b0s", [128, NJC], F32))
        b1s = e(nc.sbuf_tensor("b1s", [128, NDC], F32))
        onesrs = e(nc.sbuf_tensor("onesrs", [1, 128], F32))
        onescs = e(nc.sbuf_tensor("onescs", [128, 1], F32))
        rowm = e(nc.sbuf_tensor("rowm", [1, 512], F32))
        rowq = e(nc.sbuf_tensor("rowq", [1, 512], F32))
        rowv = e(nc.sbuf_tensor("rowv", [1, 512], F32))
        rowr = e(nc.sbuf_tensor("rowr", [1, 512], F32))
        rowmr = e(nc.sbuf_tensor("rowmr", [1, 512], F32))
        cmisc = e(nc.sbuf_tensor("cmisc", [128, len(MISC_VALS)], F32))
        _reg_misc(nc, cmisc)

        psD = [e(nc.psum_tensor(f"psD{i}", [128, 512], F32)) for i in range(2)]
        psE = [e(nc.psum_tensor(f"psE{i}", [128, 512], F32)) for i in range(2)]
        psSum = e(nc.psum_tensor("psSum", [1, 512], F32))
        psSq = e(nc.psum_tensor("psSq", [1, 512], F32))
        psB1 = e(nc.psum_tensor("psB1", [128, 512], F32))
        psB2 = e(nc.psum_tensor("psB2", [128, 512], F32))

        dX = e(nc.semaphore("dX"))
        dC = e(nc.semaphore("dC"))
        dW0 = [e(nc.semaphore(f"dW0{i}")) for i in range(2)]
        dW1 = e(nc.semaphore("dW1"))
        sPD = e(nc.semaphore("sPD"))
        sD1 = e(nc.semaphore("sD1"))
        sPE = e(nc.semaphore("sPE"))
        sEE = e(nc.semaphore("sEE"))
        sACC = e(nc.semaphore("sACC"))
        sSQ = e(nc.semaphore("sSQ"))
        sST = e(nc.semaphore("sST"))
        sRD = e(nc.semaphore("sRD"))
        sR1 = e(nc.semaphore("sR1"))
        sR2 = e(nc.semaphore("sR2"))
        sRW = e(nc.semaphore("sRW"))
        sBC = e(nc.semaphore("sBC"))
        sNM = e(nc.semaphore("sNM"))
        sCH = e(nc.semaphore("sCH"))
        sCA = e(nc.semaphore("sCA"))
        dO2 = [e(nc.semaphore(f"dO{i}")) for i in range(2)]

        block = e(nc.Block())

        @block.gpsimd
        def _(gp):
            xv = xin.rearrange("(c p) s -> p c s", p=128)
            for gi in range(4):
                gp.dma_start(out=xbf[:, gi * 4:(gi + 1) * 4, :],
                             in_=xv[:, gi * 4:(gi + 1) * 4, :]).then_inc(dX, 16)
            for sup in range(2):
                for jg in range(NG):
                    base = sup * NG + jg
                    if base > 0:
                        gp.wait_ge(sPE, 32 * base)
                    for j8 in range(NJG):
                        jc = jg * NJG + j8
                        jglob = base * NJG + j8
                        gp.dma_start(out=w1g[j8][:], in_=w1c[jc]).then_inc(dW1, 16)
                        if jglob >= 2:
                            gp.wait_ge(sD1, 2 * jglob - 2)
                        gp.dma_start(out=w0t[jglob % 2][:],
                                     in_=w0c[jc]).then_inc(dW0[jglob % 2], 16)

        @block.tensor
        def _(te):
            te.wait_ge(dX, 16 * 4)
            te.wait_ge(dC, 16 * 5)
            for sup in range(2):
                s0 = sup * SUP
                for jg in range(NG):
                    base = sup * NG + jg
                    for j8 in range(NJG):
                        jglob = base * NJG + j8
                        te.wait_ge(dW0[jglob % 2], 16 * (jglob // 2 + 1))
                        for sb in range(NSB):
                            i = jglob * NSB + sb
                            if i >= 2:
                                te.wait_ge(sD1, i - 1)
                            for hc in range(NDC):
                                ins = te.matmul(
                                    out=psD[i % 2][:],
                                    lhsT=w0t[jglob % 2][:, hc, :],
                                    rhs=xbf[:, hc, s0 + sb * 512: s0 + (sb + 1) * 512],
                                    start=(hc == 0), stop=(hc == NDC - 1))
                                if hc == NDC - 1:
                                    ins.then_inc(sPD, 1)
                    te.wait_ge(dW1, 16 * NJG * (base + 1))
                    te.wait_ge(sD1, (base * NJG + NJG) * NSB)
                    for hc in range(NDC):
                        for sb in range(NSB):
                            j = base * NDC * NSB + hc * NSB + sb
                            if j >= 2:
                                te.wait_ge(sEE, j - 1)
                            for j8 in range(NJG):
                                ins = te.matmul(
                                    out=psE[j % 2][:],
                                    lhsT=w1g[j8][:, hc * 128:(hc + 1) * 128],
                                    rhs=d1[:, j8, sb * 512:(sb + 1) * 512],
                                    start=(j8 == 0), stop=(j8 == NJG - 1))
                                if j8 == NJG - 1:
                                    ins.then_inc(sPE, 1)
                for sb in range(NSB):
                    k = sup * NSB + sb
                    te.wait_ge(sACC, k * NDC + NDC)
                    if k >= 1:
                        te.wait_ge(sRD, k)
                    for hc in range(NDC):
                        te.matmul(out=psSum[:], lhsT=onescs[:],
                                  rhs=eacc[:, hc, sb * 512:(sb + 1) * 512],
                                  start=(hc == 0), stop=(hc == NDC - 1))
                    for hc in range(NDC):
                        te.wait_ge(sSQ, k * NDC + hc + 1)
                        te.matmul(out=psSq[:], lhsT=onescs[:], rhs=sqt[hc % 2][:],
                                  start=(hc == 0), stop=(hc == NDC - 1)
                                  ).then_inc(sST, 1)
                    te.wait_ge(sRW, k + 1)
                    if k >= 1:
                        te.wait_ge(sNM, k * NDC)
                    te.matmul(out=psB1[:], lhsT=onesrs[:], rhs=rowr[:],
                              start=True, stop=True)
                    te.matmul(out=psB2[:], lhsT=onesrs[:], rhs=rowmr[:],
                              start=True, stop=True).then_inc(sBC, 1)

        @block.scalar
        def _(sc):
            sc.dma_start(out=b0s[:], in_=b0c).then_inc(dC, 16)
            sc.dma_start(out=b1s[:], in_=b1c).then_inc(dC, 16)
            sc.dma_start(out=onesrs[:], in_=onesr).then_inc(dC, 16)
            sc.dma_start(out=onescs[:], in_=onesc).then_inc(dC, 16)
            sc.dma_start(out=cmisc[:], in_=cmiscd).then_inc(dC, 16)
            sc.wait_ge(dC, 16 * 5)
            for sup in range(2):
                for jg in range(NG):
                    base = sup * NG + jg
                    for j8 in range(NJG):
                        jglob = base * NJG + j8
                        jc = jg * NJG + j8
                        for sb in range(NSB):
                            i = jglob * NSB + sb
                            sc.wait_ge(sPD, i + 1)
                            sc.activation(out=d1[:, j8, sb * 512:(sb + 1) * 512],
                                          in_=psD[i % 2][:], func=AF.Gelu,
                                          bias=b0s[:, jc:jc + 1]).then_inc(sD1, 1)
                for sb in range(NSB):
                    k = sup * NSB + sb
                    for hc in range(NDC):
                        gidx = k * NDC + hc
                        sc.wait_ge(sACC, gidx + 1)
                        if gidx >= 2:
                            sc.wait_ge(sST, gidx - 1)
                        sc.activation(out=sqt[hc % 2][:],
                                      in_=eacc[:, hc, sb * 512:(sb + 1) * 512],
                                      func=AF.Square).then_inc(sSQ, 1)
                    sc.wait_ge(sR1, k + 1)
                    sc.activation(out=rowr[:], in_=rowv[:], func=AF.Ln,
                                  bias=LN_EPS).then_inc(sCA, 1)
                    sc.wait_ge(sCA, k + 1)
                    sc.activation(out=rowr[:], in_=rowr[:], func=AF.Exp,
                                  scale=-0.5).then_inc(sR2, 1)

        @block.vector
        def _(ve):
            nch = 0

            def step(ins):
                nonlocal nch
                nch += 1
                ins.then_inc(sCH, 1)
                ve.wait_ge(sCH, nch)

            ve.wait_ge(dC, 16 * 5)
            for sup in range(2):
                s0 = sup * SUP
                for jg in range(NG):
                    base = sup * NG + jg
                    for hc in range(NDC):
                        for sb in range(NSB):
                            j = base * NDC * NSB + hc * NSB + sb
                            ve.wait_ge(sPE, j + 1)
                            dst = eacc[:, hc, sb * 512:(sb + 1) * 512]
                            if jg == 0:
                                ve.tensor_copy(out=dst, in_=psE[j % 2][:]).then_inc(sEE, 1)
                            else:
                                ve.tensor_add(out=dst, in0=dst, in1=psE[j % 2][:]).then_inc(sEE, 1)
                ve.wait_ge(sEE, 256 * (sup + 1))
                for sb in range(NSB):
                    k = sup * NSB + sb
                    for hc in range(NDC):
                        ve.scalar_tensor_tensor(
                            out=eacc[:, hc, sb * 512:(sb + 1) * 512],
                            in0=eacc[:, hc, sb * 512:(sb + 1) * 512],
                            scalar=b1s[:, hc:hc + 1],
                            in1=xbf[:, hc, s0 + sb * 512: s0 + (sb + 1) * 512],
                            op0=ALU.add, op1=ALU.add).then_inc(sACC, 1)
                    ve.wait_ge(sST, (k + 1) * NDC)
                    step(ve.tensor_scalar_mul(out=rowm[:], in0=psSum[:], scalar1=1.0 / D))
                    step(ve.tensor_scalar_mul(out=rowq[:], in0=psSq[:], scalar1=1.0 / D))
                    ve.nop().then_inc(sRD, 1)
                    step(ve.tensor_mul(out=rowv[:], in0=rowm[:], in1=rowm[:]))
                    ve.tensor_sub(out=rowv[:], in0=rowq[:], in1=rowv[:]).then_inc(sR1, 1)
                    ve.wait_ge(sR2, k + 1)
                    ve.tensor_mul(out=rowmr[:], in0=rowm[:], in1=rowr[:]).then_inc(sRW, 1)
                    ve.wait_ge(sBC, k + 1)
                    for hc in range(NDC):
                        n = k * NDC + hc
                        if n >= 2:
                            ve.wait_ge(dO2[n % 2], 16 * (n // 2))
                        step(ve.tensor_mul(out=tnrm[:],
                                           in0=eacc[:, hc, sb * 512:(sb + 1) * 512],
                                           in1=psB1[:]))
                        ve.tensor_sub(out=x2t[hc % 2][:], in0=tnrm[:],
                                      in1=psB2[:]).then_inc(sNM, 1)
                        ve.wait_ge(sNM, n + 1)

        @block.sync
        def _(sy):
            n = 0
            for sup in range(2):
                for sb in range(NSB):
                    for hc in range(NDC):
                        n += 1
                        sy.wait_ge(sNM, n)
                        sy.dma_start(
                            out=xout[hc * 128:(hc + 1) * 128,
                                     sup * SUP + sb * 512: sup * SUP + (sb + 1) * 512],
                            in_=x2t[(n - 1) % 2][:]).then_inc(dO2[(n - 1) % 2], 16)
            sy.wait_ge(dO2[0], 16 * (n // 2))
            sy.wait_ge(dO2[1], 16 * (n // 2))
    return nc


def run_l0f(g, x1T_list):
    nc = build_l0f()
    shared = {k: g[k] for k in ("w0cube", "w1cube", "b0cols", "b1cols",
                                "c_ones_row", "c_ones_col", "c_misc")}
    in_maps = [dict(shared, xin=x1T_list[b]) for b in range(NB)]
    res = run_bass_kernel_spmd(nc, in_maps, core_ids=list(range(NB)))
    return [res.results[b]["xout"] for b in range(NB)]


def build_moe():
    nc = bass.Bass("TRN2", target_bir_lowering=False)
    xin = nc.dram_tensor("xin", [D, SQ], F32, kind="ExternalInput").ap()
    wec = nc.dram_tensor("wecube", [NE, NDC, 128, D], F32, kind="ExternalInput").ap()
    gcd = nc.dram_tensor("gcube", [128, NDC, NE], F32, kind="ExternalInput").ap()
    gbd = nc.dram_tensor("gb_row", [1, NE], F32, kind="ExternalInput").ap()
    ebd = nc.dram_tensor("ebcols", [128, NE, NDC], F32, kind="ExternalInput").ap()
    oh8d = nc.dram_tensor("c_oh8", [NE, NE * 128], F32, kind="ExternalInput").ap()
    onesrd = nc.dram_tensor("c_ones_row", [1, 128], F32, kind="ExternalInput").ap()
    idd = nc.dram_tensor("c_id", [128, 128], F32, kind="ExternalInput").ap()
    cmiscd = nc.dram_tensor("c_misc", [128, len(MISC_VALS)], F32, kind="ExternalInput").ap()
    out = nc.dram_tensor("out", [SQ, D], F32, kind="ExternalOutput").ap()

    SUP = 1024
    NSC = SQ // 128

    with ExitStack() as st:
        e = st.enter_context
        xbf = e(nc.sbuf_tensor("xbf", [128, NDC, SQ], BF16))
        sel = e(nc.sbuf_tensor("sel", [128, NDC, SUP], BF16))
        wet = [e(nc.sbuf_tensor(f"wet{i}", [128, D], BF16)) for i in range(2)]
        gcs = e(nc.sbuf_tensor("gcs", [128, NDC, NE], BF16))
        gbs = e(nc.sbuf_tensor("gbs", [1, NE], BF16))
        onesrb = e(nc.sbuf_tensor("onesrb", [1, 128], BF16))
        oh8b = e(nc.sbuf_tensor("oh8b", [NE, NE * 128], BF16))
        idb = e(nc.sbuf_tensor("idb", [128, 128], BF16))
        ids = e(nc.sbuf_tensor("ids", [128, 128], F32))
        ebs = e(nc.sbuf_tensor("ebs", [128, NE, NDC], F32))
        cmisc = e(nc.sbuf_tensor("cmisc", [128, len(MISC_VALS)], F32))
        _reg_misc(nc, cmisc)
        lg = e(nc.sbuf_tensor("lg", [128, NE], F32))
        m1 = e(nc.sbuf_tensor("m1", [128, 1], F32))
        eq1 = e(nc.sbuf_tensor("eq1", [128, NE], F32))
        neg = e(nc.sbuf_tensor("neg", [128, NE], F32))
        m2 = e(nc.sbuf_tensor("m2", [128, 1], F32))
        eq2 = e(nc.sbuf_tensor("eq2", [128, NE], F32))
        wsum = e(nc.sbuf_tensor("wsum", [128, NE], F32))
        wbfT = e(nc.sbuf_tensor("wbfT", [128, NE], BF16))
        wT = e(nc.sbuf_tensor("wT", [NE, SQ], BF16))
        wbs = [e(nc.sbuf_tensor(f"wbs{i}", [128, 512], BF16)) for i in range(2)]
        Rt = [e(nc.sbuf_tensor(f"Rt{i}", [128, 512], BF16)) for i in range(2)]
        mtmp = e(nc.sbuf_tensor("mtmp", [128, 512], BF16))
        accf = [e(nc.sbuf_tensor(f"accf{i}", [128, 128], F32)) for i in range(2)]
        otok = [e(nc.sbuf_tensor(f"otok{i}", [128, D], F32)) for i in range(2)]
        sqf = [e(nc.sbuf_tensor(f"sqf{i}", [128, D], F32)) for i in range(2)]
        ofin = [e(nc.sbuf_tensor(f"ofin{i}", [128, D], F32)) for i in range(2)]
        scol = e(nc.sbuf_tensor("scol", [128, 1], F32))
        qcol = e(nc.sbuf_tensor("qcol", [128, 1], F32))
        mcol = e(nc.sbuf_tensor("mcol", [128, 1], F32))
        vcol = e(nc.sbuf_tensor("vcol", [128, 1], F32))
        rcol = e(nc.sbuf_tensor("rcol", [128, 1], F32))
        tcol = e(nc.sbuf_tensor("tcol", [128, 1], F32))

        psX = [e(nc.psum_tensor(f"psX{i}", [128, 512], F32)) for i in range(4)]
        psW = e(nc.psum_tensor("psW", [128, 512], F32))
        psG = e(nc.psum_tensor("psG", [128, 512], F32))
        psTw = e(nc.psum_tensor("psTw", [128, 256], BF16))
        psT = [psG[:, 0:128], psTw[:].bitcast(F32)[:, 0:128]]

        dX = e(nc.semaphore("dX"))
        dC = e(nc.semaphore("dC"))
        dWE = [e(nc.semaphore(f"dWE{i}")) for i in range(2)]
        sPG = e(nc.semaphore("sPG"))
        sLG = e(nc.semaphore("sLG"))
        sW1 = e(nc.semaphore("sW1"))
        sTW = e(nc.semaphore("sTW"))
        sW2 = e(nc.semaphore("sW2"))
        sWB = e(nc.semaphore("sWB"))
        sWBE = e(nc.semaphore("sWBE"))
        sXG = e(nc.semaphore("sXG"))
        sRL = e(nc.semaphore("sRL"))
        sSEL = e(nc.semaphore("sSEL"))
        sA4 = e(nc.semaphore("sA4"))
        sT4 = e(nc.semaphore("sT4"))
        sE4 = e(nc.semaphore("sE4"))
        sS4 = e(nc.semaphore("sS4"))
        sC4 = e(nc.semaphore("sC4"))
        sNM = e(nc.semaphore("sNM"))
        sCH = e(nc.semaphore("sCH"))
        sCA = e(nc.semaphore("sCA"))
        dO2 = [e(nc.semaphore(f"dO{i}")) for i in range(2)]

        block = e(nc.Block())

        @block.gpsimd
        def _(gp):
            xv = xin.rearrange("(c p) s -> p c s", p=128)
            for gi in range(4):
                gp.dma_start(out=xbf[:, gi * 4:(gi + 1) * 4, :],
                             in_=xv[:, gi * 4:(gi + 1) * 4, :]).then_inc(dX, 16)
            gp.dma_start(out=gcs[:], in_=gcd).then_inc(dX, 16)
            gp.dma_start(out=gbs[:], in_=gbd).then_inc(dX, 16)
            gp.dma_start(out=onesrb[:], in_=onesrd).then_inc(dX, 16)
            gp.dma_start(out=oh8b[:], in_=oh8d).then_inc(dX, 16)
            gp.dma_start(out=idb[:], in_=idd).then_inc(dX, 16)
            for sup in range(2):
                for ei in range(NE):
                    for oc in range(NDC):
                        idx = (sup * NE + ei) * NDC + oc
                        if idx >= 2:
                            gp.wait_ge(sXG, 2 * (idx - 1))
                        gp.dma_start(out=wet[idx % 2][:],
                                     in_=wec[ei, oc]).then_inc(dWE[idx % 2], 16)

        @block.tensor
        def _(te):
            te.wait_ge(dX, 16 * 9)
            te.wait_ge(dC, 16 * 3)
            for sc in range(NSC):
                if sc >= 1:
                    te.wait_ge(sLG, sc)
                    te.wait_ge(sW2, sc)
                for dc in range(NDC):
                    te.matmul(out=psG[:, 0:NE], lhsT=xbf[:, dc, sc * 128:(sc + 1) * 128],
                              rhs=gcs[:, dc, :], start=(dc == 0), stop=False)
                te.matmul(out=psG[:, 0:NE], lhsT=onesrb[:], rhs=gbs[:],
                          start=False, stop=True).then_inc(sPG, 1)
                te.wait_ge(sW1, sc + 1)
                te.transpose(out=psTw[0:NE, 0:128], in_=wbfT[:],
                             identity=idb[:]).then_inc(sTW, 1)
            for sup in range(2):
                s0 = sup * SUP
                for ei in range(NE):
                    eg = sup * NE + ei
                    te.wait_ge(sW2, NSC)
                    for sb in range(2):
                        if eg > 0 or sb > 0:
                            te.wait_ge(sWBE, 2 * eg + sb)
                        te.matmul(out=psW[:], lhsT=oh8b[0:NE, ei * 128:(ei + 1) * 128],
                                  rhs=wT[0:NE, s0 + sb * 512: s0 + (sb + 1) * 512],
                                  start=True, stop=True).then_inc(sWB, 1)
                    for oc in range(NDC):
                        idx = eg * NDC + oc
                        te.wait_ge(dWE[idx % 2], 16 * (idx // 2 + 1))
                        for sb in range(2):
                            j = idx * 2 + sb
                            if j >= 4:
                                te.wait_ge(sRL, j - 3)
                            for dc in range(NDC):
                                ins = te.matmul(
                                    out=psX[j % 4][:],
                                    lhsT=wet[idx % 2][:, dc * 128:(dc + 1) * 128],
                                    rhs=xbf[:, dc, s0 + sb * 512: s0 + (sb + 1) * 512],
                                    start=(dc == 0), stop=(dc == NDC - 1))
                                if dc == NDC - 1:
                                    ins.then_inc(sXG, 1)
                for sc8 in range(8):
                    sc = sup * 8 + sc8
                    for oc in range(NDC):
                        ti = sc * NDC + oc
                        te.wait_ge(sA4, ti + 1)
                        if ti >= 2:
                            te.wait_ge(sE4, ti - 1)
                        te.transpose(out=psT[ti % 2], in_=accf[ti % 2][:],
                                     identity=ids[:]).then_inc(sT4, 1)

        @block.scalar
        def _(sc_):
            nca = 0

            def astep(ins):
                nonlocal nca
                nca += 1
                ins.then_inc(sCA, 1)
                sc_.wait_ge(sCA, nca)

            sc_.dma_start(out=ebs[:], in_=ebd).then_inc(dC, 16)
            sc_.dma_start(out=ids[:], in_=idd).then_inc(dC, 16)
            sc_.dma_start(out=cmisc[:], in_=cmiscd).then_inc(dC, 16)
            sc_.wait_ge(dC, 16 * 3)
            for sup in range(2):
                for ei in range(NE):
                    eg = sup * NE + ei
                    for sb in range(2):
                        sc_.wait_ge(sWB, 2 * eg + sb + 1)
                        if eg > 0:
                            sc_.wait_ge(sSEL, 32 * eg + sb - 1)
                        sc_.activation(out=wbs[sb][:], in_=psW[:],
                                       func=AF.Copy).then_inc(sWBE, 1)
                    for oc in range(NDC):
                        idx = eg * NDC + oc
                        for sb in range(2):
                            j = idx * 2 + sb
                            sc_.wait_ge(sXG, j + 1)
                            if j >= 2:
                                sc_.wait_ge(sSEL, j - 1)
                            sc_.activation(out=Rt[j % 2][:], in_=psX[j % 4][:],
                                           func=AF.Relu,
                                           bias=ebs[:, ei, oc:oc + 1]).then_inc(sRL, 1)
                for sc8 in range(8):
                    sc = sup * 8 + sc8
                    par = sc % 2
                    for oc in range(NDC):
                        ti = sc * NDC + oc
                        sc_.wait_ge(sT4, ti + 1)
                        sc_.copy(out=otok[par][:, oc * 128:(oc + 1) * 128],
                                 in_=psT[ti % 2]).then_inc(sE4, 1)
                    sc_.wait_ge(sE4, 16 * (sc + 1))
                    astep(sc_.activation(out=sqf[par][:], in_=otok[par][:],
                                         func=AF.Copy, accum_out=scol[:]))
                    sc_.activation(out=sqf[par][:], in_=otok[par][:], func=AF.Square,
                                   accum_out=qcol[:]).then_inc(sS4, 1)
                    sc_.wait_ge(sC4, 2 * sc + 1)
                    astep(sc_.activation(out=rcol[:], in_=vcol[:], func=AF.Ln,
                                         bias=LN_EPS))
                    sc_.activation(out=rcol[:], in_=rcol[:], func=AF.Exp,
                                   scale=-0.5).then_inc(sC4, 1)

        @block.vector
        def _(ve):
            ve.wait_ge(dX, 16 * 9)
            nch = 0

            def step(ins):
                nonlocal nch
                nch += 1
                ins.then_inc(sCH, 1)
                ve.wait_ge(sCH, nch)

            for sc in range(NSC):
                ve.wait_ge(sPG, sc + 1)
                step(ve.tensor_copy(out=lg[:], in_=psG[:, 0:NE]))
                ve.nop().then_inc(sLG, 1)
                step(ve.reduce_max(out=m1[:], in_=lg[:], axis=mybir.AxisListType.X))
                step(ve.tensor_single_scalar(out=eq1[:], in_=lg[:], scalar=m1[:], op=ALU.is_ge))
                step(ve.scalar_tensor_tensor(out=neg[:], in0=eq1[:], scalar=-1e30,
                                             in1=lg[:], op0=ALU.mult, op1=ALU.add))
                step(ve.reduce_max(out=m2[:], in_=neg[:], axis=mybir.AxisListType.X))
                step(ve.tensor_single_scalar(out=eq2[:], in_=neg[:], scalar=m2[:], op=ALU.is_ge))
                step(ve.tensor_add(out=wsum[:], in0=eq1[:], in1=eq2[:]))
                ve.tensor_scalar_mul(out=wbfT[:], in0=wsum[:], scalar1=0.5).then_inc(sW1, 1)
                ve.wait_ge(sTW, sc + 1)
                ve.tensor_copy(out=wT[0:NE, sc * 128:(sc + 1) * 128],
                               in_=psTw[0:NE, 0:128]).then_inc(sW2, 1)
            for sup in range(2):
                s0 = sup * SUP
                for ei in range(NE):
                    eg = sup * NE + ei
                    for oc in range(NDC):
                        idx = eg * NDC + oc
                        for sb in range(2):
                            j = idx * 2 + sb
                            ve.wait_ge(sRL, j + 1)
                            ve.wait_ge(sWBE, 2 * eg + sb + 1)
                            dst = sel[:, oc, sb * 512:(sb + 1) * 512]
                            if ei == 0:
                                ve.tensor_mul(out=dst, in0=Rt[j % 2][:],
                                              in1=wbs[sb][:]).then_inc(sSEL, 1)
                            else:
                                step(ve.tensor_mul(out=mtmp[:], in0=Rt[j % 2][:],
                                                   in1=wbs[sb][:]))
                                ve.tensor_add(out=dst, in0=dst,
                                              in1=mtmp[:]).then_inc(sSEL, 1)
                                ve.wait_ge(sSEL, j + 1)
                for sc8 in range(8):
                    sc = sup * 8 + sc8
                    par = sc % 2
                    loc = sc8 * 128
                    for oc in range(NDC):
                        ti = sc * NDC + oc
                        if ti >= 2:
                            ve.wait_ge(sT4, ti - 1)
                        ve.tensor_add(out=accf[ti % 2][:],
                                      in0=xbf[:, oc, sc * 128:(sc + 1) * 128],
                                      in1=sel[:, oc, loc:loc + 128]).then_inc(sA4, 1)
                    ve.wait_ge(sS4, sc + 1)
                    step(ve.tensor_scalar_mul(out=mcol[:], in0=scol[:], scalar1=1.0 / D))
                    step(ve.tensor_scalar_mul(out=qcol[:], in0=qcol[:], scalar1=1.0 / D))
                    step(ve.tensor_mul(out=tcol[:], in0=mcol[:], in1=mcol[:]))
                    ve.tensor_sub(out=vcol[:], in0=qcol[:], in1=tcol[:]).then_inc(sC4, 1)
                    ve.wait_ge(sC4, 2 * sc + 2)
                    if sc >= 2:
                        ve.wait_ge(dO2[sc % 2], 16 * (sc // 2))
                    ve.tensor_scalar(out=ofin[par][:], in0=otok[par][:],
                                     scalar1=mcol[:], scalar2=rcol[:],
                                     op0=ALU.subtract, op1=ALU.mult).then_inc(sNM, 1)

        @block.sync
        def _(sy):
            for sc in range(NSC):
                sy.wait_ge(sNM, sc + 1)
                sy.dma_start(out=out[sc * 128:(sc + 1) * 128, :],
                             in_=ofin[sc % 2][:]).then_inc(dO2[sc % 2], 16)
            sy.wait_ge(dO2[0], 16 * (NSC // 2))
            sy.wait_ge(dO2[1], 16 * (NSC // 2))
    return nc


def run_moe(g, x3T_list):
    nc = build_moe()
    shared = {k: g[k] for k in ("wecube", "gcube", "gb_row", "ebcols",
                                "c_oh8", "c_ones_row", "c_id", "c_misc")}
    in_maps = [dict(shared, xin=x3T_list[b]) for b in range(NB)]
    res = run_bass_kernel_spmd(nc, in_maps, core_ids=list(range(NB)))
    return [res.results[b]["out"] for b in range(NB)]


def build_l0a():
    nc = bass.Bass("TRN2", target_bir_lowering=False)
    xin = nc.dram_tensor("xin", [D, SQ], F32, kind="ExternalInput").ap()
    wqd = nc.dram_tensor("wq", [NDC, 128, D], F32, kind="ExternalInput").ap()
    wkd = nc.dram_tensor("wk", [NDC, 128, D], F32, kind="ExternalInput").ap()
    wvd = nc.dram_tensor("wv", [NDC, 128, D], F32, kind="ExternalInput").ap()
    wod = nc.dram_tensor("wo", [NDC, 128, D], F32, kind="ExternalInput").ap()
    bqd = nc.dram_tensor("bq", [128, NDC], F32, kind="ExternalInput").ap()
    bkd = nc.dram_tensor("bk", [128, NDC], F32, kind="ExternalInput").ap()
    bod = nc.dram_tensor("bo", [128, NDC], F32, kind="ExternalInput").ap()
    bvd = nc.dram_tensor("bv_row", [1, D], F32, kind="ExternalInput").ap()
    trid = nc.dram_tensor("c_tri", [128, 128], F32, kind="ExternalInput").ap()
    idd = nc.dram_tensor("c_id", [128, 128], F32, kind="ExternalInput").ap()
    onesrd = nc.dram_tensor("c_ones_row", [1, 128], F32, kind="ExternalInput").ap()
    onescd = nc.dram_tensor("c_ones_col", [128, 1], F32, kind="ExternalInput").ap()
    cmiscd = nc.dram_tensor("c_misc", [128, len(MISC_VALS)], F32, kind="ExternalInput").ap()
    xout = nc.dram_tensor("xout", [D, SQ], BF16, kind="ExternalOutput").ap()

    BLK = 512
    NBLK = SQ // BLK
    CPB = BLK // 128

    with ExitStack() as st:
        e = st.enter_context
        xblk = [e(nc.sbuf_tensor(f"xblk{i}", [128, NDC, BLK], BF16)) for i in range(2)]
        wqt = [e(nc.sbuf_tensor(f"wqt{i}", [128, NDC, 128], BF16)) for i in range(2)]
        wkt = [e(nc.sbuf_tensor(f"wkt{i}", [128, NDC, 128], BF16)) for i in range(2)]
        wvt = [e(nc.sbuf_tensor(f"wvt{i}", [128, NDC, 512], BF16)) for i in range(2)]
        wot = [e(nc.sbuf_tensor(f"wot{i}", [128, NDC, 128], BF16)) for i in range(2)]
        qf = e(nc.sbuf_tensor("qf", [128, H, BLK], BF16))
        kf = e(nc.sbuf_tensor("kf", [128, H, BLK], BF16))
        vtok = [e(nc.sbuf_tensor(f"vtok{i}", [128, H * 129], BF16)) for i in range(2)]
        attnT = e(nc.sbuf_tensor("attnT", [128, H, BLK], BF16))
        kvf = e(nc.sbuf_tensor("kvf", [128, H * 129], F32))
        kvb = e(nc.sbuf_tensor("kvb", [128, H * 129], BF16))
        accb = e(nc.sbuf_tensor("accb", [128, NDC, BLK], BF16))
        scrA = e(nc.sbuf_tensor("scrA", [128, 3072], F32))
        scrB = scrA[:].bitcast(BF16)
        sqb = [e(nc.sbuf_tensor(f"sqb{i}", [128, BLK], BF16)) for i in range(2)]
        x1t = [e(nc.sbuf_tensor(f"x1t{i}", [128, BLK], BF16)) for i in range(2)]
        zc = [e(nc.sbuf_tensor(f"zc{i}", [128, 1], F32)) for i in range(2)]
        rz = [e(nc.sbuf_tensor(f"rz{i}", [128, 1], F32)) for i in range(2)]
        bqs = e(nc.sbuf_tensor("bqs", [128, NDC], F32))
        bks = e(nc.sbuf_tensor("bks", [128, NDC], F32))
        bos = e(nc.sbuf_tensor("bos", [128, NDC], F32))
        bvs = e(nc.sbuf_tensor("bvs", [1, D], BF16))
        tris = e(nc.sbuf_tensor("tris", [128, 128], F32))
        idb = e(nc.sbuf_tensor("idb", [128, 128], BF16))
        onesrb = e(nc.sbuf_tensor("onesrb", [1, 128], BF16))
        onesrs = e(nc.sbuf_tensor("onesrs", [1, 128], F32))
        onescb = e(nc.sbuf_tensor("onescb", [128, 1], BF16))
        rowm = e(nc.sbuf_tensor("rowm", [1, BLK], F32))
        rowq = e(nc.sbuf_tensor("rowq", [1, BLK], F32))
        rowv = e(nc.sbuf_tensor("rowv", [1, BLK], F32))
        rowr = e(nc.sbuf_tensor("rowr", [1, BLK], F32))
        rowmr = e(nc.sbuf_tensor("rowmr", [1, BLK], F32))
        cmisc = e(nc.sbuf_tensor("cmisc", [128, len(MISC_VALS)], F32))
        _reg_misc(nc, cmisc)

        def eluT(i):
            return scrA[:, i * 512:(i + 1) * 512]

        def atbf(h):
            return scrB[:, h * 128:(h + 1) * 128]

        def kft(h):
            return scrB[:, 2048 + h * 128: 2048 + (h + 1) * 128]

        def otbf(h):
            return scrB[:, 4096 + h * 128: 4096 + (h + 1) * 128]

        ps = [e(nc.psum_tensor(f"ps{i}", [128, 512], F32)) for i in range(8)]
        psQ = [ps[0], ps[1]]
        psK = [ps[2], ps[3]]
        psV = [ps[0], ps[1]]
        psA = [ps[4], ps[5]]
        psO = [ps[2], ps[3], ps[0], ps[1]]
        psT1 = ps[7][:].bitcast(BF16)[:, 0:128]
        psT2 = ps[6][:].bitcast(BF16)[:, 0:128]
        psKV = ps[7]
        psW = [ps[0], ps[1]]
        psSum = ps[4][0:1, 0:BLK]
        psSq = ps[5][0:1, 0:BLK]
        psB1 = ps[6][:, 0:BLK]
        psB2 = ps[7][:, 0:BLK]

        dX2 = [e(nc.semaphore(f"dX{i}")) for i in range(2)]
        dC = e(nc.semaphore("dC"))
        dW = e(nc.semaphore("dW"))
        dWq = [e(nc.semaphore(f"dWq{i}")) for i in range(2)]
        dWk = [e(nc.semaphore(f"dWk{i}")) for i in range(2)]
        dWv = [e(nc.semaphore(f"dWv{i}")) for i in range(2)]
        dWo = [e(nc.semaphore(f"dWo{i}")) for i in range(2)]
        dO2 = [e(nc.semaphore(f"dO{i}")) for i in range(2)]
        sINIT = e(nc.semaphore("sINIT"))
        sPQ = e(nc.semaphore("sPQ"))
        sE1 = e(nc.semaphore("sE1"))
        sPSF = e(nc.semaphore("sPSF"))
        sE2 = e(nc.semaphore("sE2"))
        sEL = e(nc.semaphore("sEL"))
        sPV = e(nc.semaphore("sPV"))
        sVE = e(nc.semaphore("sVE"))
        sPA = e(nc.semaphore("sPA"))
        sATE = e(nc.semaphore("sATE"))
        sPT1 = e(nc.semaphore("sPT1"))
        sTE1 = e(nc.semaphore("sTE1"))
        sPO = e(nc.semaphore("sPO"))
        sRZ = e(nc.semaphore("sRZ"))
        sOTE = e(nc.semaphore("sOTE"))
        sPT2 = e(nc.semaphore("sPT2"))
        sTE2 = e(nc.semaphore("sTE2"))
        sPKV = e(nc.semaphore("sPKV"))
        sKVA = e(nc.semaphore("sKVA"))
        sKVB = e(nc.semaphore("sKVB"))
        sPW = e(nc.semaphore("sPW"))
        sWE = e(nc.semaphore("sWE"))
        sSQ = e(nc.semaphore("sSQ"))
        sST = e(nc.semaphore("sST"))
        sRD = e(nc.semaphore("sRD"))
        sR1 = e(nc.semaphore("sR1"))
        sR2 = e(nc.semaphore("sR2"))
        sRW = e(nc.semaphore("sRW"))
        sBC = e(nc.semaphore("sBC"))
        sNM = e(nc.semaphore("sNM"))
        sCH = e(nc.semaphore("sCH"))
        sCA = e(nc.semaphore("sCA"))

        block = e(nc.Block())

        @block.gpsimd
        def _(gp):
            xv = xin.rearrange("(c p) s -> p c s", p=128)
            gp.dma_start(out=xblk[0][:], in_=xv[:, :, 0:BLK]).then_inc(dX2[0], 16)
            gp.dma_start(out=xblk[1][:], in_=xv[:, :, BLK:2 * BLK]).then_inc(dX2[1], 16)
            gp.dma_start(out=bvs[:], in_=bvd).then_inc(dW, 16)
            gp.dma_start(out=onesrb[:], in_=onesrd).then_inc(dW, 16)
            gp.dma_start(out=idb[:], in_=idd).then_inc(dW, 16)
            gp.dma_start(out=onescb[:], in_=onescd).then_inc(dW, 16)
            for b in range(NBLK):
                for oc in range(NDC):
                    g = b * NDC + oc
                    if g >= 2:
                        gp.wait_ge(sPQ, g - 1)
                    gp.dma_start(out=wqt[oc % 2][:],
                                 in_=wqd[:, :, oc * 128:(oc + 1) * 128].rearrange("c p j -> p c j")).then_inc(dWq[oc % 2], 16)
                    gp.dma_start(out=wkt[oc % 2][:],
                                 in_=wkd[:, :, oc * 128:(oc + 1) * 128].rearrange("c p j -> p c j")).then_inc(dWk[oc % 2], 16)
                for half in range(2):
                    hg = b * 2 + half
                    for os2 in range(4):
                        if os2 >= 2:
                            gp.wait_ge(sPV, hg * 8 + (os2 - 2) * 2 + 2)
                        elif hg >= 1:
                            gp.wait_ge(sPV, (hg - 1) * 8 + os2 * 2 + 6)
                        gp.dma_start(out=wvt[os2 % 2][:],
                                     in_=wvd[:, :, os2 * 512:(os2 + 1) * 512].rearrange("c p j -> p c j")).then_inc(dWv[os2 % 2], 16)
                for oc in range(NDC):
                    g = b * NDC + oc
                    if g >= 2:
                        gp.wait_ge(sPW, g - 1)
                    gp.dma_start(out=wot[oc % 2][:],
                                 in_=wod[:, :, oc * 128:(oc + 1) * 128].rearrange("c p j -> p c j")).then_inc(dWo[oc % 2], 16)
                if b + 2 < NBLK:
                    gp.wait_ge(sWE, 16 * (b + 1))
                    gp.dma_start(out=xblk[b % 2][:],
                                 in_=xv[:, :, (b + 2) * BLK:(b + 3) * BLK]).then_inc(dX2[b % 2], 16)

        @block.tensor
        def _(te):
            te.wait_ge(dC, 16 * 6)
            te.wait_ge(dW, 16 * 4)
            te.wait_ge(sINIT, 4)
            for b in range(NBLK):
                xb = xblk[b % 2]
                te.wait_ge(dX2[b % 2], 16 * (b // 2 + 1))
                if b > 0:
                    te.wait_ge(sWE, 16 * b)
                    te.wait_ge(sOTE, 64 * b)
                for oc in range(NDC):
                    g = b * NDC + oc
                    te.wait_ge(dWq[oc % 2], 16 * (b * 8 + oc // 2 + 1))
                    te.wait_ge(dWk[oc % 2], 16 * (b * 8 + oc // 2 + 1))
                    if g >= 2:
                        te.wait_ge(sE1, g - 1)
                        te.wait_ge(sPSF, g - 1)
                    for dc in range(NDC):
                        te.matmul(out=psQ[oc % 2][:],
                                  lhsT=wqt[oc % 2][:, dc, :],
                                  rhs=xb[:, dc, :],
                                  start=(dc == 0), stop=(dc == NDC - 1))
                    for dc in range(NDC):
                        ins = te.matmul(out=psK[oc % 2][:],
                                        lhsT=wkt[oc % 2][:, dc, :],
                                        rhs=xb[:, dc, :],
                                        start=(dc == 0), stop=(dc == NDC - 1))
                        if dc == NDC - 1:
                            ins.then_inc(sPQ, 1)
                te.wait_ge(sE1, b * NDC + NDC)
                te.wait_ge(sPSF, b * NDC + NDC)
                if b > 0:
                    te.wait_ge(sRD, b)
                    te.wait_ge(sNM, 16 * b)
                for half in range(2):
                    hg = b * 2 + half
                    for os2 in range(4):
                        te.wait_ge(dWv[os2 % 2], 16 * (hg * 2 + os2 // 2 + 1))
                        for ch2 in range(2):
                            g2 = hg * 8 + os2 * 2 + ch2
                            ch = half * 2 + ch2
                            if g2 >= 2:
                                te.wait_ge(sVE, g2 - 1)
                            for dc in range(NDC):
                                te.matmul(out=psV[g2 % 2][:],
                                          lhsT=xb[:, dc, ch * 128:(ch + 1) * 128],
                                          rhs=wvt[os2 % 2][:, dc, :],
                                          start=(dc == 0), stop=False)
                            te.matmul(out=psV[g2 % 2][:], lhsT=onesrb[:],
                                      rhs=bvs[0:1, os2 * 512:(os2 + 1) * 512],
                                      start=False, stop=True).then_inc(sPV, 1)
                    for ch2 in range(2):
                        ch = half * 2 + ch2
                        cgb = b * 64 + ch * 16
                        te.wait_ge(sKVA, cgb)
                        for h in range(H):
                            cg = cgb + h
                            te.wait_ge(sEL, b * NDC + h + 1)
                            if cg >= 2:
                                te.wait_ge(sATE, cg - 1)
                            te.matmul(out=psA[cg % 2][:, 0:128],
                                      lhsT=kf[:, h, ch * 128:(ch + 1) * 128],
                                      rhs=qf[:, h, ch * 128:(ch + 1) * 128],
                                      start=True, stop=True).then_inc(sPA, 1)
                            te.wait_ge(sTE1, cg)
                            te.transpose(out=psT1, in_=kf[:, h, ch * 128:(ch + 1) * 128],
                                         identity=idb[:]).then_inc(sPT1, 1)
                        te.wait_ge(sVE, hg * 8 + 8)
                        for h in range(H):
                            cg = cgb + h
                            te.wait_ge(sATE, cg + 1)
                            if cg >= 4:
                                te.wait_ge(sOTE, cg - 3)
                            if cg >= 16:
                                te.wait_ge(sKVB, cg - 15)
                            te.matmul(out=psO[cg % 4][:, 0:129],
                                      lhsT=atbf(h),
                                      rhs=vtok[ch % 2][:, h * 129:(h + 1) * 129],
                                      start=True, stop=False)
                            te.matmul(out=psO[cg % 4][:, 0:129],
                                      lhsT=qf[:, h, ch * 128:(ch + 1) * 128],
                                      rhs=kvb[:, h * 129:(h + 1) * 129],
                                      start=False, stop=True).then_inc(sPO, 1)
                        for h in range(H):
                            cg = cgb + h
                            te.wait_ge(sOTE, cg + 1)
                            te.wait_ge(sTE2, cg)
                            te.transpose(out=psT2, in_=otbf(h),
                                         identity=idb[:]).then_inc(sPT2, 1)
                        te.wait_ge(sTE1, cgb + 16)
                        for h in range(H):
                            cg = cgb + h
                            te.wait_ge(sKVA, cg)
                            te.matmul(out=psKV[:, 0:129],
                                      lhsT=kft(h),
                                      rhs=vtok[ch % 2][:, h * 129:(h + 1) * 129],
                                      start=True, stop=True).then_inc(sPKV, 1)
                te.wait_ge(sTE2, 64 * (b + 1))
                te.wait_ge(sVE, b * 16 + 16)
                for oc in range(NDC):
                    g = b * NDC + oc
                    te.wait_ge(dWo[oc % 2], 16 * (b * 8 + oc // 2 + 1))
                    if g >= 2:
                        te.wait_ge(sWE, g - 1)
                    for hec in range(NDC):
                        ins = te.matmul(out=psW[oc % 2][:],
                                        lhsT=wot[oc % 2][:, hec, :],
                                        rhs=attnT[:, hec, :],
                                        start=(hec == 0), stop=(hec == NDC - 1))
                        if hec == NDC - 1:
                            ins.then_inc(sPW, 1)
                te.wait_ge(sWE, b * NDC + NDC)
                te.wait_ge(sATE, 64 * (b + 1))
                te.wait_ge(sKVA, 64 * (b + 1))
                for oc in range(NDC):
                    te.matmul(out=psSum, lhsT=onescb[:], rhs=accb[:, oc, :],
                              start=(oc == 0), stop=(oc == NDC - 1))
                for oc in range(NDC):
                    te.wait_ge(sSQ, b * NDC + oc + 1)
                    te.matmul(out=psSq, lhsT=onescb[:], rhs=sqb[oc % 2][:],
                              start=(oc == 0), stop=(oc == NDC - 1)).then_inc(sST, 1)
                te.wait_ge(sRW, b + 1)
                if b > 0:
                    te.wait_ge(sNM, 16 * b)
                te.matmul(out=psB1, lhsT=onesrs[:], rhs=rowr[0:1, :],
                          start=True, stop=True)
                te.matmul(out=psB2, lhsT=onesrs[:], rhs=rowmr[0:1, :],
                          start=True, stop=True).then_inc(sBC, 1)

        @block.scalar
        def _(sc):
            sc.dma_start(out=bqs[:], in_=bqd).then_inc(dC, 16)
            sc.dma_start(out=bks[:], in_=bkd).then_inc(dC, 16)
            sc.dma_start(out=bos[:], in_=bod).then_inc(dC, 16)
            sc.dma_start(out=tris[:], in_=trid).then_inc(dC, 16)
            sc.dma_start(out=onesrs[:], in_=onesrd).then_inc(dC, 16)
            sc.dma_start(out=cmisc[:], in_=cmiscd).then_inc(dC, 16)
            sc.wait_ge(dC, 16 * 6)
            nca = 0
            for b in range(NBLK):
                for oc in range(NDC):
                    g = b * NDC + oc
                    sc.wait_ge(sPQ, g + 1)
                    sc.wait_ge(sE1, g + 1)
                    sc.activation(out=eluT(2), in_=psQ[oc % 2][:],
                                  func=AF.Relu, bias=bqs[:, oc:oc + 1])
                    sc.activation(out=eluT(3), in_=psK[oc % 2][:],
                                  func=AF.Relu, bias=bks[:, oc:oc + 1]).then_inc(sPSF, 1)
                    sc.activation(out=eluT(4), in_=eluT(0), func=AF.Exp)
                    sc.activation(out=eluT(5), in_=eluT(1),
                                  func=AF.Exp).then_inc(sE2, 1)
                for half in range(2):
                    hg = b * 2 + half
                    for os2 in range(4):
                        for ch2 in range(2):
                            g2 = hg * 8 + os2 * 2 + ch2
                            ch = half * 2 + ch2
                            gci = b * CPB + ch
                            sc.wait_ge(sPV, g2 + 1)
                            if gci >= 2:
                                sc.wait_ge(sPKV, 16 * (gci - 1))
                            for hh in range(4):
                                h = os2 * 4 + hh
                                ins = sc.activation(
                                    out=vtok[ch % 2][:, h * 129: h * 129 + 128],
                                    in_=psV[g2 % 2][:, hh * 128:(hh + 1) * 128],
                                    func=AF.Copy)
                                if hh == 3:
                                    ins.then_inc(sVE, 1)
                    for ch2 in range(2):
                        ch = half * 2 + ch2
                        cgb = b * 64 + ch * 16
                        for h in range(H):
                            cg = cgb + h
                            sc.wait_ge(sRZ, cg + 1)
                            if cg >= 16:
                                sc.wait_ge(sPT2, cg - 15)
                            sc.activation(out=otbf(h), in_=psO[cg % 4][:, 0:128],
                                          func=AF.Copy, scale=rz[cg % 2][:]).then_inc(sOTE, 1)
                        for h in range(H):
                            cg = cgb + h
                            sc.wait_ge(sKVA, cg + 1)
                            sc.activation(out=kvb[:, h * 129:(h + 1) * 129],
                                          in_=kvf[:, h * 129:(h + 1) * 129],
                                          func=AF.Copy).then_inc(sKVB, 1)
                sc.wait_ge(sR1, b + 1)
                nca += 1
                sc.activation(out=rowr[0:1, :], in_=rowv[0:1, :], func=AF.Ln,
                              bias=LN_EPS).then_inc(sCA, 1)
                sc.wait_ge(sCA, nca)
                sc.activation(out=rowr[0:1, :], in_=rowr[0:1, :], func=AF.Exp,
                              scale=-0.5).then_inc(sR2, 1)

        @block.vector
        def _(ve):
            nch = 0

            def step(ins):
                nonlocal nch
                nch += 1
                ins.then_inc(sCH, 1)
                ve.wait_ge(sCH, nch)

            ve.wait_ge(dC, 16 * 6)
            ve.memset(kvf[:], 0.0).then_inc(sINIT, 1)
            ve.memset(kvb[:], 0.0).then_inc(sINIT, 1)
            for i in range(2):
                vv = vtok[i][:].rearrange("p (h c) -> p h c", c=129)
                ve.memset(vv[:, :, 128:129], 1.0).then_inc(sINIT, 1)
            for b in range(NBLK):
                xb = xblk[b % 2]
                for oc in range(NDC + 1):
                    g = b * NDC + oc
                    if oc >= 1:
                        po = oc - 1
                        ve.wait_ge(sE2, b * NDC + po + 1)
                        ve.tensor_add(out=qf[:, po, :], in0=eluT(2), in1=eluT(4))
                        ins = ve.tensor_add(out=kf[:, po, :], in0=eluT(3),
                                            in1=eluT(5))
                        if po == NDC - 1:
                            nch += 1
                            ins.then_inc(sCH, 1)
                            ve.wait_ge(sCH, nch)
                            ve.nop().then_inc(sEL, 1)
                        else:
                            ins.then_inc(sEL, 1)
                    if oc < NDC:
                        ve.wait_ge(sPQ, g + 1)
                        if g >= 1:
                            ve.wait_ge(sE2, g)
                        ve.tensor_scalar(out=eluT(0), in0=psQ[oc % 2][:],
                                         scalar1=bqs[:, oc:oc + 1], scalar2=0.0,
                                         op0=ALU.add, op1=ALU.min)
                        ve.tensor_scalar(out=eluT(1), in0=psK[oc % 2][:],
                                         scalar1=bks[:, oc:oc + 1], scalar2=0.0,
                                         op0=ALU.add, op1=ALU.min).then_inc(sE1, 1)
                for ch in range(CPB):
                    cgb = b * 64 + ch * 16
                    for h in range(H):
                        cg = cgb + h
                        ve.wait_ge(sPA, cg + 1)
                        ve.tensor_mul(out=atbf(h), in0=psA[cg % 2][:, 0:128],
                                      in1=tris[:]).then_inc(sATE, 1)
                        ve.wait_ge(sPT1, cg + 1)
                        ve.tensor_copy(out=kft(h), in_=psT1).then_inc(sTE1, 1)
                    for h in range(H):
                        cg = cgb + h
                        ve.wait_ge(sPO, cg + 1)
                        if cg >= 2:
                            ve.wait_ge(sOTE, cg - 1)
                        step(ve.tensor_single_scalar(out=zc[cg % 2][:],
                                                     in_=psO[cg % 4][:, 128:129],
                                                     scalar=ATT_EPS, op=ALU.add))
                        ve.reciprocal(out=rz[cg % 2][:], in_=zc[cg % 2][:]).then_inc(sRZ, 1)
                    for h in range(H):
                        cg = cgb + h
                        ve.wait_ge(sPT2, cg + 1)
                        ve.tensor_copy(out=attnT[:, h, ch * 128:(ch + 1) * 128],
                                       in_=psT2).then_inc(sTE2, 1)
                    for h in range(H):
                        cg = cgb + h
                        ve.wait_ge(sPKV, cg + 1)
                        ve.tensor_add(out=kvf[:, h * 129:(h + 1) * 129],
                                      in0=kvf[:, h * 129:(h + 1) * 129],
                                      in1=psKV[:, 0:129]).then_inc(sKVA, 1)
                for oc in range(NDC):
                    g = b * NDC + oc
                    ve.wait_ge(sPW, g + 1)
                    ve.scalar_tensor_tensor(out=accb[:, oc, :], in0=psW[oc % 2][:],
                                            scalar=bos[:, oc:oc + 1],
                                            in1=xb[:, oc, :],
                                            op0=ALU.add, op1=ALU.add).then_inc(sWE, 1)
                for oc in range(NDC):
                    g = b * NDC + oc
                    if g >= 2:
                        ve.wait_ge(sST, g - 1)
                    ve.tensor_mul(out=sqb[oc % 2][:], in0=accb[:, oc, :],
                                  in1=accb[:, oc, :]).then_inc(sSQ, 1)
                ve.wait_ge(sST, (b + 1) * NDC)
                step(ve.tensor_scalar_mul(out=rowm[0:1, :], in0=psSum, scalar1=1.0 / D))
                step(ve.tensor_scalar_mul(out=rowq[0:1, :], in0=psSq, scalar1=1.0 / D))
                ve.nop().then_inc(sRD, 1)
                step(ve.tensor_mul(out=rowv[0:1, :], in0=rowm[0:1, :], in1=rowm[0:1, :]))
                ve.tensor_sub(out=rowv[0:1, :], in0=rowq[0:1, :],
                              in1=rowv[0:1, :]).then_inc(sR1, 1)
                ve.wait_ge(sR2, b + 1)
                ve.tensor_mul(out=rowmr[0:1, :], in0=rowm[0:1, :],
                              in1=rowr[0:1, :]).then_inc(sRW, 1)
                ve.wait_ge(sBC, b + 1)
                for oc in range(NDC):
                    n = b * NDC + oc
                    if n >= 2:
                        ve.wait_ge(dO2[n % 2], 16 * (n // 2))
                    step(ve.tensor_mul(out=x1t[oc % 2][:], in0=accb[:, oc, :],
                                       in1=psB1))
                    ve.tensor_sub(out=x1t[oc % 2][:], in0=x1t[oc % 2][:],
                                  in1=psB2).then_inc(sNM, 1)

        @block.sync
        def _(sy):
            n = 0
            for b in range(NBLK):
                for oc in range(NDC):
                    n += 1
                    sy.wait_ge(sNM, n)
                    sy.dma_start(out=xout[oc * 128:(oc + 1) * 128,
                                          b * BLK:(b + 1) * BLK],
                                 in_=x1t[(n - 1) % 2][:]).then_inc(dO2[(n - 1) % 2], 16)
            sy.wait_ge(dO2[0], 16 * (n // 2))
            sy.wait_ge(dO2[1], 16 * (n // 2))
    return nc


def run_l0a(g):
    nc = build_l0a()
    shared = {k: g[k] for k in ("wq", "wk", "wv", "wo", "bq", "bk", "bo", "bv_row",
                                "c_tri", "c_id", "c_ones_row", "c_ones_col", "c_misc")}
    in_maps = [dict(shared, xin=g["xT"][b]) for b in range(NB)]
    res = run_bass_kernel_spmd(nc, in_maps, core_ids=list(range(NB)))
    return [res.results[b]["xout"] for b in range(NB)]


def build_l1a():
    nc = bass.Bass("TRN2", target_bir_lowering=False)
    xin = nc.dram_tensor("xin", [D, SQ], F32, kind="ExternalInput").ap()
    wdtind = nc.dram_tensor("wdtin", [128, NDC, DTR], F32, kind="ExternalInput").ap()
    wdtoutd = nc.dram_tensor("wdtout", [DTR, D], F32, kind="ExternalInput").ap()
    wBd = nc.dram_tensor("wB", [128, NDC, STN], F32, kind="ExternalInput").ap()
    wCd = nc.dram_tensor("wC", [128, NDC, STN], F32, kind="ExternalInput").ap()
    acolsd = nc.dram_tensor("acols", [128, NDC, STN], F32, kind="ExternalInput").ap()
    dcolsd = nc.dram_tensor("dcols", [128, NDC], F32, kind="ExternalInput").ap()
    bdtd = nc.dram_tensor("bdtcols", [128, NDC], F32, kind="ExternalInput").ap()
    oh16d = nc.dram_tensor("c_oh16", [STN, STN * 128], F32, kind="ExternalInput").ap()
    onesrd = nc.dram_tensor("c_ones_row", [1, 128], F32, kind="ExternalInput").ap()
    onescd = nc.dram_tensor("c_ones_col", [128, 1], F32, kind="ExternalInput").ap()
    cmiscd = nc.dram_tensor("c_misc", [128, len(MISC_VALS)], F32, kind="ExternalInput").ap()
    xout = nc.dram_tensor("xout", [D, SQ], F32, kind="ExternalOutput").ap()

    TC = 512
    NTC = SQ // TC

    with ExitStack() as st:
        e = st.enter_context
        xbf = e(nc.sbuf_tensor("xbf", [128, NDC, SQ], BF16))
        dt1b = e(nc.sbuf_tensor("dt1b", [DTR, SQ], BF16))
        wdtinb = e(nc.sbuf_tensor("wdtinb", [128, NDC, DTR], BF16))
        wdtoutb = e(nc.sbuf_tensor("wdtoutb", [DTR, D], BF16))
        wBb = e(nc.sbuf_tensor("wBb", [128, NDC, STN], BF16))
        wCb = e(nc.sbuf_tensor("wCb", [128, NDC, STN], BF16))
        oh16b = e(nc.sbuf_tensor("oh16b", [STN, STN * 128], BF16))
        onesrs = e(nc.sbuf_tensor("onesrs", [1, 128], F32))
        onescb = e(nc.sbuf_tensor("onescb", [128, 1], BF16))
        acolsf = e(nc.sbuf_tensor("acolsf", [128, NDC, STN], F32))
        aneg = acolsf[:].rearrange("p c n -> p (c n)")
        dcolsf = e(nc.sbuf_tensor("dcolsf", [128, NDC], F32))
        dp1 = e(nc.sbuf_tensor("dp1", [128, NDC], F32))
        bdtf = e(nc.sbuf_tensor("bdtf", [128, NDC], F32))
        Bsb = e(nc.sbuf_tensor("Bsb", [STN, TC], BF16))
        Csb = e(nc.sbuf_tensor("Csb", [STN, TC], BF16))
        Bb = e(nc.sbuf_tensor("Bb", [128, STN, TC], BF16))
        CBb = e(nc.sbuf_tensor("CBb", [128, STN, TC], BF16))
        dtc = [e(nc.sbuf_tensor(f"dtc{i}", [128, TC], F32)) for i in range(2)]
        abuf = [e(nc.sbuf_tensor(f"abuf{i}", [128, STN, TC], BF16)) for i in range(2)]
        dtu = [e(nc.sbuf_tensor(f"dtu{i}", [128, TC], BF16)) for i in range(2)]
        dtuB = e(nc.sbuf_tensor("dtuB", [128, STN, TC], BF16))
        ytmp = e(nc.sbuf_tensor("ytmp", [128, TC], F32))
        s0 = e(nc.sbuf_tensor("s0", [128, NDC * STN], F32))
        accb = e(nc.sbuf_tensor("accb", [128, NDC, TC], BF16))
        t1t = e(nc.sbuf_tensor("t1t", [128, TC], F32))
        t2t = e(nc.sbuf_tensor("t2t", [128, TC], F32))
        t4t = e(nc.sbuf_tensor("t4t", [128, TC], F32))
        sqL = [e(nc.sbuf_tensor(f"sqL{i}", [128, TC], BF16)) for i in range(2)]
        x3t = [e(nc.sbuf_tensor(f"x3t{i}", [128, TC], F32)) for i in range(2)]
        rowm = e(nc.sbuf_tensor("rowm", [1, TC], F32))
        rowq = e(nc.sbuf_tensor("rowq", [1, TC], F32))
        rowv = e(nc.sbuf_tensor("rowv", [1, TC], F32))
        rowr = e(nc.sbuf_tensor("rowr", [1, TC], F32))
        cmisc = e(nc.sbuf_tensor("cmisc", [128, len(MISC_VALS)], F32))
        _reg_misc(nc, cmisc)

        ps = [e(nc.psum_tensor(f"ps{i}", [128, 512], F32)) for i in range(8)]
        psDt = [ps[0], ps[1]]
        psBC = ps[2]
        psBr = [ps[3], ps[4]]
        psLNr = ps[7][:, 0:TC]
        psLNm = ps[6][:, 0:TC]
        psSum = ps[5][0:1, 0:TC]
        psSq = ps[6][0:1, 0:TC]

        dX = e(nc.semaphore("dX"))
        dC = e(nc.semaphore("dC"))
        dW = e(nc.semaphore("dW"))
        sPR = e(nc.semaphore("sPR"))
        sD1 = e(nc.semaphore("sD1"))
        sD1E = e(nc.semaphore("sD1E"))
        sPB = e(nc.semaphore("sPB"))
        sBCE = e(nc.semaphore("sBCE"))
        sBB = e(nc.semaphore("sBB"))
        sBBE = e(nc.semaphore("sBBE"))
        sPD = e(nc.semaphore("sPD"))
        sSP = e(nc.semaphore("sSP"))
        sSPF = e(nc.semaphore("sSPF"))
        sDT = e(nc.semaphore("sDT"))
        sAG = e(nc.semaphore("sAG"))
        sSN = e(nc.semaphore("sSN"))
        sSCM = e(nc.semaphore("sSCM"))
        sACC = e(nc.semaphore("sACC"))
        sSQ = e(nc.semaphore("sSQ"))
        sST = e(nc.semaphore("sST"))
        sRD = e(nc.semaphore("sRD"))
        sR1 = e(nc.semaphore("sR1"))
        sR2 = e(nc.semaphore("sR2"))
        sRW = e(nc.semaphore("sRW"))
        sBCst = e(nc.semaphore("sBCst"))
        sNM = e(nc.semaphore("sNM"))
        sCH = e(nc.semaphore("sCH"))
        sCA = e(nc.semaphore("sCA"))
        dO2 = [e(nc.semaphore(f"dO{i}")) for i in range(2)]

        block = e(nc.Block())

        @block.gpsimd
        def _(gp):
            xv = xin.rearrange("(c p) s -> p c s", p=128)
            for gi in range(4):
                gp.dma_start(out=xbf[:, gi * 4:(gi + 1) * 4, :],
                             in_=xv[:, gi * 4:(gi + 1) * 4, :]).then_inc(dX, 16)
            gp.dma_start(out=wdtinb[:], in_=wdtind).then_inc(dW, 16)
            gp.dma_start(out=wdtoutb[:], in_=wdtoutd).then_inc(dW, 16)
            gp.dma_start(out=wBb[:], in_=wBd).then_inc(dW, 16)
            gp.dma_start(out=wCb[:], in_=wCd).then_inc(dW, 16)
            gp.dma_start(out=oh16b[:], in_=oh16d).then_inc(dW, 16)
            gp.dma_start(out=onescb[:], in_=onescd).then_inc(dW, 16)

        NW = 6

        @block.tensor
        def _(te):
            te.wait_ge(dX, 16 * 4)
            te.wait_ge(dW, 16 * NW)
            te.wait_ge(dC, 16 * 5)
            for sb in range(4):
                if sb >= 2:
                    te.wait_ge(sD1E, sb - 1)
                for hc in range(NDC):
                    ins = te.matmul(out=psDt[sb % 2][0:DTR, :],
                                    lhsT=wdtinb[:, hc, :],
                                    rhs=xbf[:, hc, sb * 512:(sb + 1) * 512],
                                    start=(hc == 0), stop=(hc == NDC - 1))
                    if hc == NDC - 1:
                        ins.then_inc(sD1, 1)
            def pe_ln(tc):
                te.wait_ge(sACC, 16 * (tc + 1))
                if tc > 0:
                    te.wait_ge(sRD, tc)
                for hc in range(NDC):
                    te.matmul(out=psSum, lhsT=onescb[:], rhs=accb[:, hc, :],
                              start=(hc == 0), stop=(hc == NDC - 1))
                if tc > 0:
                    te.wait_ge(sNM, 16 * tc)
                for hc in range(NDC):
                    te.wait_ge(sSQ, tc * NDC + hc + 1)
                    te.matmul(out=psSq, lhsT=onescb[:], rhs=sqL[hc % 2][:],
                              start=(hc == 0), stop=(hc == NDC - 1)).then_inc(sST, 1)
                te.wait_ge(sRW, tc + 1)
                te.matmul(out=psLNr, lhsT=onesrs[:], rhs=rowr[0:1, :],
                          start=True, stop=True)
                te.matmul(out=psLNm, lhsT=onesrs[:], rhs=rowq[0:1, :],
                          start=True, stop=True).then_inc(sBCst, 1)

            for tc in range(NTC):
                t0 = tc * TC
                for bc in range(2):
                    te.wait_ge(sBCE, 2 * tc + bc)
                    w = wBb if bc == 0 else wCb
                    for hc in range(NDC):
                        ins = te.matmul(out=psBC[0:STN, :],
                                        lhsT=w[:, hc, :],
                                        rhs=xbf[:, hc, t0:t0 + TC],
                                        start=(hc == 0), stop=(hc == NDC - 1))
                        if hc == NDC - 1:
                            ins.then_inc(sPB, 1)
                for i in range(2 * STN):
                    gb = tc * 2 * STN + i
                    te.wait_ge(sBCE, 2 * tc + (1 if i < STN else 2))
                    if gb >= 2:
                        te.wait_ge(sBBE, gb - 1)
                    src = Bsb if i < STN else Csb
                    n = i % STN
                    te.matmul(out=psBr[gb % 2][:],
                              lhsT=oh16b[0:STN, n * 128:(n + 1) * 128],
                              rhs=src[0:STN, :],
                              start=True, stop=True).then_inc(sBB, 1)
                if tc == 0:
                    te.wait_ge(sD1E, 4)
                for hc in range(NDC):
                    ghc = tc * NDC + hc
                    if ghc >= 2:
                        te.wait_ge(sSP, ghc - 1)
                    te.matmul(out=psDt[ghc % 2][:],
                              lhsT=wdtoutb[0:DTR, hc * 128:(hc + 1) * 128],
                              rhs=dt1b[0:DTR, t0:t0 + TC],
                              start=True, stop=True).then_inc(sPD, 1)
                    if hc == 0 and tc > 0:
                        pe_ln(tc - 1)
            pe_ln(NTC - 1)

        @block.scalar
        def _(sc):
            sc.dma_start(out=acolsf[:], in_=acolsd).then_inc(dC, 16)
            sc.dma_start(out=dcolsf[:], in_=dcolsd).then_inc(dC, 16)
            sc.dma_start(out=bdtf[:], in_=bdtd).then_inc(dC, 16)
            sc.dma_start(out=onesrs[:], in_=onesrd).then_inc(dC, 16)
            sc.dma_start(out=cmisc[:], in_=cmiscd).then_inc(dC, 16)
            sc.wait_ge(dC, 16 * 5)
            nca = 0

            def astep(ins):
                nonlocal nca
                nca += 1
                ins.then_inc(sCA, 1)
                sc.wait_ge(sCA, nca)

            astep(sc.activation(out=aneg, in_=aneg, func=AF.Exp))
            for sb in range(4):
                sc.wait_ge(sD1, sb + 1)
                sc.activation(out=dt1b[0:DTR, sb * 512:(sb + 1) * 512],
                              in_=psDt[sb % 2][0:DTR, :], func=AF.Copy).then_inc(sD1E, 1)
            for tc in range(NTC):
                t0 = tc * TC
                for bc in range(2):
                    sc.wait_ge(sPB, 2 * tc + bc + 1)
                    if tc > 0:
                        sc.wait_ge(sBB, 32 * tc)
                    dst = Bsb if bc == 0 else Csb
                    sc.activation(out=dst[0:STN, :], in_=psBC[0:STN, :],
                                  func=AF.Copy).then_inc(sBCE, 1)
                if tc > 0:
                    sc.wait_ge(sSCM, NDC * tc)
                for i in range(2 * STN):
                    gb = tc * 2 * STN + i
                    sc.wait_ge(sBB, gb + 1)
                    n = i % STN
                    dst = Bb if i < STN else CBb
                    sc.activation(out=dst[:, n, :], in_=psBr[gb % 2][:],
                                  func=AF.Copy).then_inc(sBBE, 1)
                for hc in range(NDC):
                    ghc = tc * NDC + hc
                    sc.wait_ge(sPD, ghc + 1)
                    if ghc >= 1:
                        sc.wait_ge(sDT, ghc)
                    astep(sc.activation(out=t1t[:], in_=psDt[ghc % 2][:],
                                        func=AF.Abs, bias=bdtf[:, hc:hc + 1]))
                    sc.activation(out=t4t[:], in_=psDt[ghc % 2][:],
                                  func=AF.Relu, bias=bdtf[:, hc:hc + 1]).then_inc(sSP, 1)
                    astep(sc.activation(out=t2t[:], in_=t1t[:], func=AF.Exp,
                                        scale=-1.0))
                    sc.activation(out=t1t[:], in_=t2t[:], func=AF.Ln,
                                  bias=1.0).then_inc(sSPF, 1)
                    if hc == 0 and tc > 0:
                        sc.wait_ge(sR1, tc)
                        astep(sc.activation(out=rowr[0:1, :], in_=rowv[0:1, :],
                                            func=AF.Ln, bias=LN_EPS))
                        sc.activation(out=rowr[0:1, :], in_=rowr[0:1, :], func=AF.Exp,
                                      scale=-0.5).then_inc(sR2, 1)
                    sc.wait_ge(sPR, 1)
                    sc.wait_ge(sDT, ghc + 1)
                    if ghc >= 2:
                        sc.wait_ge(sSCM, ghc - 1)
                    for n in range(STN):
                        ins = sc.activation(out=abuf[ghc % 2][:, n, :],
                                            in_=dtc[ghc % 2][:], func=AF.Exp,
                                            scale=aneg[:, hc * STN + n: hc * STN + n + 1])
                        if n == STN - 1:
                            ins.then_inc(sAG, 1)
            sc.wait_ge(sR1, NTC)
            astep(sc.activation(out=rowr[0:1, :], in_=rowv[0:1, :], func=AF.Ln,
                                bias=LN_EPS))
            sc.activation(out=rowr[0:1, :], in_=rowr[0:1, :], func=AF.Exp,
                          scale=-0.5).then_inc(sR2, 1)

        @block.vector
        def _(ve):
            nch = 0

            def step(ins):
                nonlocal nch
                nch += 1
                ins.then_inc(sCH, 1)
                ve.wait_ge(sCH, nch)

            ve.wait_ge(dC, 16 * 5)
            ve.wait_ge(dW, 16 * NW)
            ve.wait_ge(sCA, 1)
            ve.tensor_scalar_mul(out=aneg, in0=aneg, scalar1=-1.0).then_inc(sPR, 1)
            ve.tensor_scalar_add(out=dp1[:], in0=dcolsf[:], scalar1=1.0).then_inc(sPR, 1)
            ve.wait_ge(sPR, 2)
            ve.memset(s0[:], 0.0).then_inc(sPR, 1)
            NGH = NTC * NDC

            def ln_chunk(tc):
                t0 = tc * TC
                for hc in range(NDC):
                    g = tc * NDC + hc
                    if g >= 2:
                        ve.wait_ge(sST, g - 1)
                    ve.tensor_mul(out=sqL[hc % 2][:], in0=accb[:, hc, :],
                                  in1=accb[:, hc, :]).then_inc(sSQ, 1)
                ve.wait_ge(sST, (tc + 1) * NDC)
                step(ve.tensor_scalar_mul(out=rowm[0:1, :], in0=psSum, scalar1=1.0 / D))
                step(ve.tensor_scalar_mul(out=rowq[0:1, :], in0=psSq, scalar1=1.0 / D))
                ve.nop().then_inc(sRD, 1)
                step(ve.tensor_mul(out=rowv[0:1, :], in0=rowm[0:1, :], in1=rowm[0:1, :]))
                ve.tensor_sub(out=rowv[0:1, :], in0=rowq[0:1, :],
                              in1=rowv[0:1, :]).then_inc(sR1, 1)
                ve.wait_ge(sR2, tc + 1)
                ve.tensor_mul(out=rowq[0:1, :], in0=rowm[0:1, :],
                              in1=rowr[0:1, :]).then_inc(sRW, 1)
                ve.wait_ge(sBCst, tc + 1)
                for hc in range(NDC):
                    nn = tc * NDC + hc
                    if nn >= 2:
                        ve.wait_ge(dO2[nn % 2], 16 * (nn // 2))
                    step(ve.tensor_mul(out=x3t[hc % 2][:], in0=accb[:, hc, :],
                                       in1=psLNr))
                    ve.tensor_sub(out=x3t[hc % 2][:], in0=x3t[hc % 2][:],
                                  in1=psLNm).then_inc(sNM, 1)

            def front(k):
                tck, hck = divmod(k, NDC)
                t0k = tck * TC
                ve.wait_ge(sSPF, k + 1)
                if k >= 2:
                    ve.wait_ge(sAG, k - 1)
                step(ve.tensor_add(out=dtc[k % 2][:], in0=t1t[:], in1=t4t[:]))
                ve.nop().then_inc(sDT, 1)
                step(ve.tensor_mul(out=dtu[k % 2][:], in0=dtc[k % 2][:],
                                   in1=xbf[:, hck, t0k:t0k + TC]))

            for k in range(NGH + 1):
                boundary = (k % NDC == 0)
                if k < NGH and not boundary:
                    front(k)
                if k >= 1:
                    j = k - 1
                    tcj, hcj = divmod(j, NDC)
                    t0j = tcj * TC
                    ve.wait_ge(sAG, j + 1)
                    for n in range(STN):
                        ins = ve.tensor_tensor_scan(
                            out=abuf[j % 2][:, n, :],
                            data0=abuf[j % 2][:, n, :],
                            data1=dtuB[:, n, :],
                            initial=s0[:, hcj * STN + n: hcj * STN + n + 1],
                            op0=ALU.mult, op1=ALU.add)
                        if n < STN - 1:
                            ins.then_inc(sSN, 1)
                        else:
                            nch += 1
                            ins.then_inc(sCH, 1)
                            ve.wait_ge(sCH, nch)
                            ve.nop().then_inc(sSN, 1)
                    step(ve.tensor_copy(
                        out=s0[:, hcj * STN:(hcj + 1) * STN],
                        in_=abuf[j % 2][:, :, TC - 1:TC].rearrange("p n o -> p (n o)")))
                    ve.tensor_mul(out=dtuB[:], in0=abuf[j % 2][:],
                                  in1=CBb[:]).then_inc(sSCM, 1)
                    ve.wait_ge(sSCM, j + 1)
                    step(ve.tensor_add(out=dtuB[:, 0:8, :], in0=dtuB[:, 0:8, :],
                                       in1=dtuB[:, 8:16, :]))
                    step(ve.tensor_add(out=dtuB[:, 0:4, :], in0=dtuB[:, 0:4, :],
                                       in1=dtuB[:, 4:8, :]))
                    step(ve.tensor_add(out=dtuB[:, 0:2, :], in0=dtuB[:, 0:2, :],
                                       in1=dtuB[:, 2:4, :]))
                    step(ve.tensor_add(out=ytmp[:], in0=dtuB[:, 0, :],
                                       in1=dtuB[:, 1, :]))
                    step(ve.scalar_tensor_tensor(out=accb[:, hcj, :],
                                                 in0=xbf[:, hcj, t0j:t0j + TC],
                                                 scalar=dp1[:, hcj:hcj + 1],
                                                 in1=ytmp[:],
                                                 op0=ALU.mult, op1=ALU.add))
                    ve.nop().then_inc(sACC, 1)
                    if hcj == NDC - 1:
                        ln_chunk(tcj)
                if k < NGH:
                    if boundary:
                        front(k)
                    bcast_dtu = dtu[k % 2][:].rearrange("p (a t) -> p a t", a=1).broadcast_to([128, STN, TC])
                    step(ve.tensor_mul(out=dtuB[:], in0=Bb[:], in1=bcast_dtu))

        @block.sync
        def _(sy):
            n = 0
            for tc in range(NTC):
                for hc in range(NDC):
                    n += 1
                    sy.wait_ge(sNM, n)
                    sy.dma_start(out=xout[hc * 128:(hc + 1) * 128,
                                          tc * TC:(tc + 1) * TC],
                                 in_=x3t[(n - 1) % 2][:]).then_inc(dO2[(n - 1) % 2], 16)
            sy.wait_ge(dO2[0], 16 * (n // 2))
            sy.wait_ge(dO2[1], 16 * (n // 2))
    return nc


def run_l1a(g, x2T_list):
    nc = build_l1a()
    shared = {k: g[k] for k in ("wdtin", "wdtout", "wB", "wC", "acols", "dcols",
                                "bdtcols", "c_oh16", "c_ones_row", "c_ones_col",
                                "c_misc")}
    in_maps = [dict(shared, xin=x2T_list[b]) for b in range(NB)]
    res = run_bass_kernel_spmd(nc, in_maps, core_ids=list(range(NB)))
    return [res.results[b]["xout"] for b in range(NB)]


def kernel_unfused(**inputs):
    g = prep_host(inputs)
    x1 = run_l0a(g)
    x2 = run_l0f(g, x1)
    x3 = run_l1a(g, x2)
    out = run_moe(g, x3)
    return np.stack([np.asarray(o, np.float32) for o in out])


def emit_l0a(nc, E):
    xin = E["xin"]
    wqd = E["wq"]
    wkd = E["wk"]
    wvd = E["wv"]
    wod = E["wo"]
    bqd = E["bq"]
    bkd = E["bk"]
    bod = E["bo"]
    bvd = E["bv_row"]
    trid = E["c_tri"]
    idd = E["c_id"]
    onesrd = E["c_ones_row"]
    onescd = E["c_ones_col"]
    cmiscd = E["c_misc"]
    xout = E["xout"]

    BLK = 512
    NBLK = SQ // BLK
    CPB = BLK // 128

    with ExitStack() as st:
        e = st.enter_context
        xblk = [e(nc.sbuf_tensor(f"a0xblk{i}", [128, NDC, BLK], BF16)) for i in range(2)]
        wqt = [e(nc.sbuf_tensor(f"a0wqt{i}", [128, NDC, 128], BF16)) for i in range(2)]
        wkt = [e(nc.sbuf_tensor(f"a0wkt{i}", [128, NDC, 128], BF16)) for i in range(2)]
        wvt = [e(nc.sbuf_tensor(f"a0wvt{i}", [128, NDC, 512], BF16)) for i in range(2)]
        wot = [e(nc.sbuf_tensor(f"a0wot{i}", [128, NDC, 128], BF16)) for i in range(2)]
        qf = e(nc.sbuf_tensor("a0qf", [128, H, BLK], BF16))
        kf = e(nc.sbuf_tensor("a0kf", [128, H, BLK], BF16))
        vtok = [e(nc.sbuf_tensor(f"a0vtok{i}", [128, H * 129], BF16)) for i in range(2)]
        attnT = e(nc.sbuf_tensor("a0attnT", [128, H, BLK], BF16))
        kvf = e(nc.sbuf_tensor("a0kvf", [128, H * 129], F32))
        kvb = e(nc.sbuf_tensor("a0kvb", [128, H * 129], BF16))
        accb = e(nc.sbuf_tensor("a0accb", [128, NDC, BLK], BF16))
        scrA = e(nc.sbuf_tensor("a0scrA", [128, 3072], F32))
        scrB = scrA[:].bitcast(BF16)
        sqb = [e(nc.sbuf_tensor(f"a0sqb{i}", [128, BLK], BF16)) for i in range(2)]
        x1t = [e(nc.sbuf_tensor(f"a0x1t{i}", [128, BLK], BF16)) for i in range(2)]
        zc = [e(nc.sbuf_tensor(f"a0zc{i}", [128, 1], F32)) for i in range(2)]
        rz = [e(nc.sbuf_tensor(f"a0rz{i}", [128, 1], F32)) for i in range(2)]
        bqs = e(nc.sbuf_tensor("a0bqs", [128, NDC], F32))
        bks = e(nc.sbuf_tensor("a0bks", [128, NDC], F32))
        bos = e(nc.sbuf_tensor("a0bos", [128, NDC], F32))
        bvs = e(nc.sbuf_tensor("a0bvs", [1, D], BF16))
        tris = e(nc.sbuf_tensor("a0tris", [128, 128], F32))
        idb = e(nc.sbuf_tensor("a0idb", [128, 128], BF16))
        onesrb = e(nc.sbuf_tensor("a0onesrb", [1, 128], BF16))
        onesrs = e(nc.sbuf_tensor("a0onesrs", [1, 128], F32))
        onescb = e(nc.sbuf_tensor("a0onescb", [128, 1], BF16))
        rowm = e(nc.sbuf_tensor("a0rowm", [1, BLK], F32))
        rowq = e(nc.sbuf_tensor("a0rowq", [1, BLK], F32))
        rowv = e(nc.sbuf_tensor("a0rowv", [1, BLK], F32))
        rowr = e(nc.sbuf_tensor("a0rowr", [1, BLK], F32))
        rowmr = e(nc.sbuf_tensor("a0rowmr", [1, BLK], F32))
        cmisc = e(nc.sbuf_tensor("a0cmisc", [128, len(MISC_VALS)], F32))
        _reg_misc(nc, cmisc)

        def eluT(i):
            return scrA[:, i * 512:(i + 1) * 512]

        def atbf(h):
            return scrB[:, h * 128:(h + 1) * 128]

        def kft(h):
            return scrB[:, 2048 + h * 128: 2048 + (h + 1) * 128]

        def otbf(h):
            return scrB[:, 4096 + h * 128: 4096 + (h + 1) * 128]

        ps = [e(nc.psum_tensor(f"a0ps{i}", [128, 512], F32)) for i in range(8)]
        psQ = [ps[0], ps[1]]
        psK = [ps[2], ps[3]]
        psV = [ps[0], ps[1]]
        psA = [ps[4], ps[5]]
        psO = [ps[2], ps[3], ps[0], ps[1]]
        psT1 = ps[7][:].bitcast(BF16)[:, 0:128]
        psT2 = ps[6][:].bitcast(BF16)[:, 0:128]
        psKV = ps[7]
        psW = [ps[0], ps[1]]
        psSum = ps[4][0:1, 0:BLK]
        psSq = ps[5][0:1, 0:BLK]
        psB1 = ps[6][:, 0:BLK]
        psB2 = ps[7][:, 0:BLK]

        dX2 = [e(nc.semaphore(f"a0dX{i}")) for i in range(2)]
        dC = e(nc.semaphore("a0dC"))
        dW = e(nc.semaphore("a0dW"))
        dWq = [e(nc.semaphore(f"a0dWq{i}")) for i in range(2)]
        dWk = [e(nc.semaphore(f"a0dWk{i}")) for i in range(2)]
        dWv = [e(nc.semaphore(f"a0dWv{i}")) for i in range(2)]
        dWo = [e(nc.semaphore(f"a0dWo{i}")) for i in range(2)]
        dO2 = [e(nc.semaphore(f"a0dO{i}")) for i in range(2)]
        sINIT = e(nc.semaphore("a0sINIT"))
        sPQ = e(nc.semaphore("a0sPQ"))
        sE1 = e(nc.semaphore("a0sE1"))
        sPSF = e(nc.semaphore("a0sPSF"))
        sE2 = e(nc.semaphore("a0sE2"))
        sEL = e(nc.semaphore("a0sEL"))
        sPV = e(nc.semaphore("a0sPV"))
        sVE = e(nc.semaphore("a0sVE"))
        sPA = e(nc.semaphore("a0sPA"))
        sATE = e(nc.semaphore("a0sATE"))
        sPT1 = e(nc.semaphore("a0sPT1"))
        sTE1 = e(nc.semaphore("a0sTE1"))
        sPO = e(nc.semaphore("a0sPO"))
        sRZ = e(nc.semaphore("a0sRZ"))
        sOTE = e(nc.semaphore("a0sOTE"))
        sPT2 = e(nc.semaphore("a0sPT2"))
        sTE2 = e(nc.semaphore("a0sTE2"))
        sPKV = e(nc.semaphore("a0sPKV"))
        sKVA = e(nc.semaphore("a0sKVA"))
        sKVB = e(nc.semaphore("a0sKVB"))
        sPW = e(nc.semaphore("a0sPW"))
        sWE = e(nc.semaphore("a0sWE"))
        sSQ = e(nc.semaphore("a0sSQ"))
        sST = e(nc.semaphore("a0sST"))
        sRD = e(nc.semaphore("a0sRD"))
        sR1 = e(nc.semaphore("a0sR1"))
        sR2 = e(nc.semaphore("a0sR2"))
        sRW = e(nc.semaphore("a0sRW"))
        sBC = e(nc.semaphore("a0sBC"))
        sNM = e(nc.semaphore("a0sNM"))
        sCH = e(nc.semaphore("a0sCH"))
        sCA = e(nc.semaphore("a0sCA"))

        block = e(nc.Block())

        @block.gpsimd
        def _(gp):
            xv = xin.rearrange("(c p) s -> p c s", p=128)
            gp.dma_start(out=xblk[0][:], in_=xv[:, :, 0:BLK]).then_inc(dX2[0], 16)
            gp.dma_start(out=xblk[1][:], in_=xv[:, :, BLK:2 * BLK]).then_inc(dX2[1], 16)
            gp.dma_start(out=bvs[:], in_=bvd).then_inc(dW, 16)
            gp.dma_start(out=onesrb[:], in_=onesrd).then_inc(dW, 16)
            gp.dma_start(out=idb[:], in_=idd).then_inc(dW, 16)
            gp.dma_start(out=onescb[:], in_=onescd).then_inc(dW, 16)
            for b in range(NBLK):
                for oc in range(NDC):
                    g = b * NDC + oc
                    if g >= 2:
                        gp.wait_ge(sPQ, g - 1)
                    gp.dma_start(out=wqt[oc % 2][:],
                                 in_=wqd[:, :, oc * 128:(oc + 1) * 128].rearrange("c p j -> p c j")).then_inc(dWq[oc % 2], 16)
                    gp.dma_start(out=wkt[oc % 2][:],
                                 in_=wkd[:, :, oc * 128:(oc + 1) * 128].rearrange("c p j -> p c j")).then_inc(dWk[oc % 2], 16)
                for half in range(2):
                    hg = b * 2 + half
                    for os2 in range(4):
                        if os2 >= 2:
                            gp.wait_ge(sPV, hg * 8 + (os2 - 2) * 2 + 2)
                        elif hg >= 1:
                            gp.wait_ge(sPV, (hg - 1) * 8 + os2 * 2 + 6)
                        gp.dma_start(out=wvt[os2 % 2][:],
                                     in_=wvd[:, :, os2 * 512:(os2 + 1) * 512].rearrange("c p j -> p c j")).then_inc(dWv[os2 % 2], 16)
                for oc in range(NDC):
                    g = b * NDC + oc
                    if g >= 2:
                        gp.wait_ge(sPW, g - 1)
                    gp.dma_start(out=wot[oc % 2][:],
                                 in_=wod[:, :, oc * 128:(oc + 1) * 128].rearrange("c p j -> p c j")).then_inc(dWo[oc % 2], 16)
                if b + 2 < NBLK:
                    gp.wait_ge(sWE, 16 * (b + 1))
                    gp.dma_start(out=xblk[b % 2][:],
                                 in_=xv[:, :, (b + 2) * BLK:(b + 3) * BLK]).then_inc(dX2[b % 2], 16)

        @block.tensor
        def _(te):
            te.wait_ge(dC, 16 * 6)
            te.wait_ge(dW, 16 * 4)
            te.wait_ge(sINIT, 4)
            for b in range(NBLK):
                xb = xblk[b % 2]
                te.wait_ge(dX2[b % 2], 16 * (b // 2 + 1))
                if b > 0:
                    te.wait_ge(sWE, 16 * b)
                    te.wait_ge(sOTE, 64 * b)
                for oc in range(NDC):
                    g = b * NDC + oc
                    te.wait_ge(dWq[oc % 2], 16 * (b * 8 + oc // 2 + 1))
                    te.wait_ge(dWk[oc % 2], 16 * (b * 8 + oc // 2 + 1))
                    if g >= 2:
                        te.wait_ge(sE1, g - 1)
                        te.wait_ge(sPSF, g - 1)
                    for dc in range(NDC):
                        te.matmul(out=psQ[oc % 2][:],
                                  lhsT=wqt[oc % 2][:, dc, :],
                                  rhs=xb[:, dc, :],
                                  start=(dc == 0), stop=(dc == NDC - 1))
                    for dc in range(NDC):
                        ins = te.matmul(out=psK[oc % 2][:],
                                        lhsT=wkt[oc % 2][:, dc, :],
                                        rhs=xb[:, dc, :],
                                        start=(dc == 0), stop=(dc == NDC - 1))
                        if dc == NDC - 1:
                            ins.then_inc(sPQ, 1)
                te.wait_ge(sE1, b * NDC + NDC)
                te.wait_ge(sPSF, b * NDC + NDC)
                if b > 0:
                    te.wait_ge(sRD, b)
                    te.wait_ge(sNM, 16 * b)
                for half in range(2):
                    hg = b * 2 + half
                    for os2 in range(4):
                        te.wait_ge(dWv[os2 % 2], 16 * (hg * 2 + os2 // 2 + 1))
                        for ch2 in range(2):
                            g2 = hg * 8 + os2 * 2 + ch2
                            ch = half * 2 + ch2
                            if g2 >= 2:
                                te.wait_ge(sVE, g2 - 1)
                            for dc in range(NDC):
                                te.matmul(out=psV[g2 % 2][:],
                                          lhsT=xb[:, dc, ch * 128:(ch + 1) * 128],
                                          rhs=wvt[os2 % 2][:, dc, :],
                                          start=(dc == 0), stop=False)
                            te.matmul(out=psV[g2 % 2][:], lhsT=onesrb[:],
                                      rhs=bvs[0:1, os2 * 512:(os2 + 1) * 512],
                                      start=False, stop=True).then_inc(sPV, 1)
                    for ch2 in range(2):
                        ch = half * 2 + ch2
                        cgb = b * 64 + ch * 16
                        te.wait_ge(sKVA, cgb)
                        for h in range(H):
                            cg = cgb + h
                            te.wait_ge(sEL, b * NDC + h + 1)
                            if cg >= 2:
                                te.wait_ge(sATE, cg - 1)
                            te.matmul(out=psA[cg % 2][:, 0:128],
                                      lhsT=kf[:, h, ch * 128:(ch + 1) * 128],
                                      rhs=qf[:, h, ch * 128:(ch + 1) * 128],
                                      start=True, stop=True).then_inc(sPA, 1)
                            te.wait_ge(sTE1, cg)
                            te.transpose(out=psT1, in_=kf[:, h, ch * 128:(ch + 1) * 128],
                                         identity=idb[:]).then_inc(sPT1, 1)
                        te.wait_ge(sVE, hg * 8 + 8)
                        for h in range(H):
                            cg = cgb + h
                            te.wait_ge(sATE, cg + 1)
                            if cg >= 4:
                                te.wait_ge(sOTE, cg - 3)
                            if cg >= 16:
                                te.wait_ge(sKVB, cg - 15)
                            te.matmul(out=psO[cg % 4][:, 0:129],
                                      lhsT=atbf(h),
                                      rhs=vtok[ch % 2][:, h * 129:(h + 1) * 129],
                                      start=True, stop=False)
                            te.matmul(out=psO[cg % 4][:, 0:129],
                                      lhsT=qf[:, h, ch * 128:(ch + 1) * 128],
                                      rhs=kvb[:, h * 129:(h + 1) * 129],
                                      start=False, stop=True).then_inc(sPO, 1)
                        for h in range(H):
                            cg = cgb + h
                            te.wait_ge(sOTE, cg + 1)
                            te.wait_ge(sTE2, cg)
                            te.transpose(out=psT2, in_=otbf(h),
                                         identity=idb[:]).then_inc(sPT2, 1)
                        te.wait_ge(sTE1, cgb + 16)
                        for h in range(H):
                            cg = cgb + h
                            te.wait_ge(sKVA, cg)
                            te.matmul(out=psKV[:, 0:129],
                                      lhsT=kft(h),
                                      rhs=vtok[ch % 2][:, h * 129:(h + 1) * 129],
                                      start=True, stop=True).then_inc(sPKV, 1)
                te.wait_ge(sTE2, 64 * (b + 1))
                te.wait_ge(sVE, b * 16 + 16)
                for oc in range(NDC):
                    g = b * NDC + oc
                    te.wait_ge(dWo[oc % 2], 16 * (b * 8 + oc // 2 + 1))
                    if g >= 2:
                        te.wait_ge(sWE, g - 1)
                    for hec in range(NDC):
                        ins = te.matmul(out=psW[oc % 2][:],
                                        lhsT=wot[oc % 2][:, hec, :],
                                        rhs=attnT[:, hec, :],
                                        start=(hec == 0), stop=(hec == NDC - 1))
                        if hec == NDC - 1:
                            ins.then_inc(sPW, 1)
                te.wait_ge(sWE, b * NDC + NDC)
                te.wait_ge(sATE, 64 * (b + 1))
                te.wait_ge(sKVA, 64 * (b + 1))
                for oc in range(NDC):
                    te.matmul(out=psSum, lhsT=onescb[:], rhs=accb[:, oc, :],
                              start=(oc == 0), stop=(oc == NDC - 1))
                for oc in range(NDC):
                    te.wait_ge(sSQ, b * NDC + oc + 1)
                    te.matmul(out=psSq, lhsT=onescb[:], rhs=sqb[oc % 2][:],
                              start=(oc == 0), stop=(oc == NDC - 1)).then_inc(sST, 1)
                te.wait_ge(sRW, b + 1)
                if b > 0:
                    te.wait_ge(sNM, 16 * b)
                te.matmul(out=psB1, lhsT=onesrs[:], rhs=rowr[0:1, :],
                          start=True, stop=True)
                te.matmul(out=psB2, lhsT=onesrs[:], rhs=rowmr[0:1, :],
                          start=True, stop=True).then_inc(sBC, 1)

        @block.scalar
        def _(sc):
            sc.dma_start(out=bqs[:], in_=bqd).then_inc(dC, 16)
            sc.dma_start(out=bks[:], in_=bkd).then_inc(dC, 16)
            sc.dma_start(out=bos[:], in_=bod).then_inc(dC, 16)
            sc.dma_start(out=tris[:], in_=trid).then_inc(dC, 16)
            sc.dma_start(out=onesrs[:], in_=onesrd).then_inc(dC, 16)
            sc.dma_start(out=cmisc[:], in_=cmiscd).then_inc(dC, 16)
            sc.wait_ge(dC, 16 * 6)
            nca = 0
            for b in range(NBLK):
                for oc in range(NDC):
                    g = b * NDC + oc
                    sc.wait_ge(sPQ, g + 1)
                    sc.wait_ge(sE1, g + 1)
                    sc.activation(out=eluT(2), in_=psQ[oc % 2][:],
                                  func=AF.Relu, bias=bqs[:, oc:oc + 1])
                    sc.activation(out=eluT(3), in_=psK[oc % 2][:],
                                  func=AF.Relu, bias=bks[:, oc:oc + 1]).then_inc(sPSF, 1)
                    sc.activation(out=eluT(4), in_=eluT(0), func=AF.Exp)
                    sc.activation(out=eluT(5), in_=eluT(1),
                                  func=AF.Exp).then_inc(sE2, 1)
                for half in range(2):
                    hg = b * 2 + half
                    for os2 in range(4):
                        for ch2 in range(2):
                            g2 = hg * 8 + os2 * 2 + ch2
                            ch = half * 2 + ch2
                            gci = b * CPB + ch
                            sc.wait_ge(sPV, g2 + 1)
                            if gci >= 2:
                                sc.wait_ge(sPKV, 16 * (gci - 1))
                            for hh in range(4):
                                h = os2 * 4 + hh
                                ins = sc.activation(
                                    out=vtok[ch % 2][:, h * 129: h * 129 + 128],
                                    in_=psV[g2 % 2][:, hh * 128:(hh + 1) * 128],
                                    func=AF.Copy)
                                if hh == 3:
                                    ins.then_inc(sVE, 1)
                    for ch2 in range(2):
                        ch = half * 2 + ch2
                        cgb = b * 64 + ch * 16
                        for h in range(H):
                            cg = cgb + h
                            sc.wait_ge(sRZ, cg + 1)
                            if cg >= 16:
                                sc.wait_ge(sPT2, cg - 15)
                            sc.activation(out=otbf(h), in_=psO[cg % 4][:, 0:128],
                                          func=AF.Copy, scale=rz[cg % 2][:]).then_inc(sOTE, 1)
                        for h in range(H):
                            cg = cgb + h
                            sc.wait_ge(sKVA, cg + 1)
                            sc.activation(out=kvb[:, h * 129:(h + 1) * 129],
                                          in_=kvf[:, h * 129:(h + 1) * 129],
                                          func=AF.Copy).then_inc(sKVB, 1)
                sc.wait_ge(sR1, b + 1)
                nca += 1
                sc.activation(out=rowr[0:1, :], in_=rowv[0:1, :], func=AF.Ln,
                              bias=LN_EPS).then_inc(sCA, 1)
                sc.wait_ge(sCA, nca)
                sc.activation(out=rowr[0:1, :], in_=rowr[0:1, :], func=AF.Exp,
                              scale=-0.5).then_inc(sR2, 1)

        @block.vector
        def _(ve):
            nch = 0

            def step(ins):
                nonlocal nch
                nch += 1
                ins.then_inc(sCH, 1)
                ve.wait_ge(sCH, nch)

            ve.wait_ge(dC, 16 * 6)
            ve.memset(kvf[:], 0.0).then_inc(sINIT, 1)
            ve.memset(kvb[:], 0.0).then_inc(sINIT, 1)
            for i in range(2):
                vv = vtok[i][:].rearrange("p (h c) -> p h c", c=129)
                ve.memset(vv[:, :, 128:129], 1.0).then_inc(sINIT, 1)
            for b in range(NBLK):
                xb = xblk[b % 2]
                for oc in range(NDC + 1):
                    g = b * NDC + oc
                    if oc >= 1:
                        po = oc - 1
                        ve.wait_ge(sE2, b * NDC + po + 1)
                        ve.tensor_add(out=qf[:, po, :], in0=eluT(2), in1=eluT(4))
                        ins = ve.tensor_add(out=kf[:, po, :], in0=eluT(3),
                                            in1=eluT(5))
                        if po == NDC - 1:
                            nch += 1
                            ins.then_inc(sCH, 1)
                            ve.wait_ge(sCH, nch)
                            ve.nop().then_inc(sEL, 1)
                        else:
                            ins.then_inc(sEL, 1)
                    if oc < NDC:
                        ve.wait_ge(sPQ, g + 1)
                        if g >= 1:
                            ve.wait_ge(sE2, g)
                        ve.tensor_scalar(out=eluT(0), in0=psQ[oc % 2][:],
                                         scalar1=bqs[:, oc:oc + 1], scalar2=0.0,
                                         op0=ALU.add, op1=ALU.min)
                        ve.tensor_scalar(out=eluT(1), in0=psK[oc % 2][:],
                                         scalar1=bks[:, oc:oc + 1], scalar2=0.0,
                                         op0=ALU.add, op1=ALU.min).then_inc(sE1, 1)
                for ch in range(CPB):
                    cgb = b * 64 + ch * 16
                    for h in range(H):
                        cg = cgb + h
                        ve.wait_ge(sPA, cg + 1)
                        ve.tensor_mul(out=atbf(h), in0=psA[cg % 2][:, 0:128],
                                      in1=tris[:]).then_inc(sATE, 1)
                        ve.wait_ge(sPT1, cg + 1)
                        ve.tensor_copy(out=kft(h), in_=psT1).then_inc(sTE1, 1)
                    for h in range(H):
                        cg = cgb + h
                        ve.wait_ge(sPO, cg + 1)
                        if cg >= 2:
                            ve.wait_ge(sOTE, cg - 1)
                        step(ve.tensor_single_scalar(out=zc[cg % 2][:],
                                                     in_=psO[cg % 4][:, 128:129],
                                                     scalar=ATT_EPS, op=ALU.add))
                        ve.reciprocal(out=rz[cg % 2][:], in_=zc[cg % 2][:]).then_inc(sRZ, 1)
                    for h in range(H):
                        cg = cgb + h
                        ve.wait_ge(sPT2, cg + 1)
                        ve.tensor_copy(out=attnT[:, h, ch * 128:(ch + 1) * 128],
                                       in_=psT2).then_inc(sTE2, 1)
                    for h in range(H):
                        cg = cgb + h
                        ve.wait_ge(sPKV, cg + 1)
                        ve.tensor_add(out=kvf[:, h * 129:(h + 1) * 129],
                                      in0=kvf[:, h * 129:(h + 1) * 129],
                                      in1=psKV[:, 0:129]).then_inc(sKVA, 1)
                for oc in range(NDC):
                    g = b * NDC + oc
                    ve.wait_ge(sPW, g + 1)
                    ve.scalar_tensor_tensor(out=accb[:, oc, :], in0=psW[oc % 2][:],
                                            scalar=bos[:, oc:oc + 1],
                                            in1=xb[:, oc, :],
                                            op0=ALU.add, op1=ALU.add).then_inc(sWE, 1)
                for oc in range(NDC):
                    g = b * NDC + oc
                    if g >= 2:
                        ve.wait_ge(sST, g - 1)
                    ve.tensor_mul(out=sqb[oc % 2][:], in0=accb[:, oc, :],
                                  in1=accb[:, oc, :]).then_inc(sSQ, 1)
                ve.wait_ge(sST, (b + 1) * NDC)
                step(ve.tensor_scalar_mul(out=rowm[0:1, :], in0=psSum, scalar1=1.0 / D))
                step(ve.tensor_scalar_mul(out=rowq[0:1, :], in0=psSq, scalar1=1.0 / D))
                ve.nop().then_inc(sRD, 1)
                step(ve.tensor_mul(out=rowv[0:1, :], in0=rowm[0:1, :], in1=rowm[0:1, :]))
                ve.tensor_sub(out=rowv[0:1, :], in0=rowq[0:1, :],
                              in1=rowv[0:1, :]).then_inc(sR1, 1)
                ve.wait_ge(sR2, b + 1)
                ve.tensor_mul(out=rowmr[0:1, :], in0=rowm[0:1, :],
                              in1=rowr[0:1, :]).then_inc(sRW, 1)
                ve.wait_ge(sBC, b + 1)
                for oc in range(NDC):
                    n = b * NDC + oc
                    if n >= 2:
                        ve.wait_ge(dO2[n % 2], 16 * (n // 2))
                    step(ve.tensor_mul(out=x1t[oc % 2][:], in0=accb[:, oc, :],
                                       in1=psB1))
                    ve.tensor_sub(out=x1t[oc % 2][:], in0=x1t[oc % 2][:],
                                  in1=psB2).then_inc(sNM, 1)

        @block.sync
        def _(sy):
            n = 0
            for b in range(NBLK):
                for oc in range(NDC):
                    n += 1
                    sy.wait_ge(sNM, n)
                    sy.dma_start(out=xout[oc * 128:(oc + 1) * 128,
                                          b * BLK:(b + 1) * BLK],
                                 in_=x1t[(n - 1) % 2][:]).then_inc(dO2[(n - 1) % 2], 16)
            sy.wait_ge(dO2[0], 16 * (n // 2))
            sy.wait_ge(dO2[1], 16 * (n // 2))


def emit_l0f(nc, E):
    xin = E["xin"]
    w0c = E["w0cube"]
    w1c = E["w1cube"]
    b0c = E["b0cols"]
    b1c = E["b1cols"]
    onesr = E["c_ones_row"]
    onesc = E["c_ones_col"]
    cmiscd = E["c_misc"]
    xout = E["xout"]

    NJG = 8
    NG = NJC // NJG
    SUP = 1024
    NSB = SUP // 512

    with ExitStack() as st:
        e = st.enter_context
        xbf = E["xshared"]
        eacc = e(nc.sbuf_tensor("f0eacc", [128, NDC, SUP], F32))
        w0t = [e(nc.sbuf_tensor(f"f0w0t{i}", [128, NDC, 128], BF16)) for i in range(2)]
        w1g = [e(nc.sbuf_tensor(f"f0w1g{i}", [128, D], BF16)) for i in range(NJG)]
        d1 = e(nc.sbuf_tensor("f0d1", [128, NJG, SUP], BF16))
        sqt = [e(nc.sbuf_tensor(f"f0sqt{i}", [128, 512], F32)) for i in range(2)]
        x2t = [e(nc.sbuf_tensor(f"f0x2t{i}", [128, 512], F32)) for i in range(2)]
        tnrm = e(nc.sbuf_tensor("f0tnrm", [128, 512], F32))
        b0s = e(nc.sbuf_tensor("f0b0s", [128, NJC], F32))
        b1s = e(nc.sbuf_tensor("f0b1s", [128, NDC], F32))
        onesrs = e(nc.sbuf_tensor("f0onesrs", [1, 128], F32))
        onescs = e(nc.sbuf_tensor("f0onescs", [128, 1], F32))
        rowm = e(nc.sbuf_tensor("f0rowm", [1, 512], F32))
        rowq = e(nc.sbuf_tensor("f0rowq", [1, 512], F32))
        rowv = e(nc.sbuf_tensor("f0rowv", [1, 512], F32))
        rowr = e(nc.sbuf_tensor("f0rowr", [1, 512], F32))
        rowmr = e(nc.sbuf_tensor("f0rowmr", [1, 512], F32))
        cmisc = e(nc.sbuf_tensor("f0cmisc", [128, len(MISC_VALS)], F32))
        _reg_misc(nc, cmisc)

        psD = [e(nc.psum_tensor(f"f0psD{i}", [128, 512], F32)) for i in range(2)]
        psE = [e(nc.psum_tensor(f"f0psE{i}", [128, 512], F32)) for i in range(2)]
        psSum = e(nc.psum_tensor("f0psSum", [1, 512], F32))
        psSq = e(nc.psum_tensor("f0psSq", [1, 512], F32))
        psB1 = e(nc.psum_tensor("f0psB1", [128, 512], F32))
        psB2 = e(nc.psum_tensor("f0psB2", [128, 512], F32))

        dX = e(nc.semaphore("f0dX"))
        dC = e(nc.semaphore("f0dC"))
        dW0 = [e(nc.semaphore(f"f0dW0{i}")) for i in range(2)]
        dW1 = e(nc.semaphore("f0dW1"))
        sPD = e(nc.semaphore("f0sPD"))
        sD1 = e(nc.semaphore("f0sD1"))
        sPE = e(nc.semaphore("f0sPE"))
        sEE = e(nc.semaphore("f0sEE"))
        sACC = e(nc.semaphore("f0sACC"))
        sSQ = e(nc.semaphore("f0sSQ"))
        sST = e(nc.semaphore("f0sST"))
        sRD = e(nc.semaphore("f0sRD"))
        sR1 = e(nc.semaphore("f0sR1"))
        sR2 = e(nc.semaphore("f0sR2"))
        sRW = e(nc.semaphore("f0sRW"))
        sBC = e(nc.semaphore("f0sBC"))
        sNM = e(nc.semaphore("f0sNM"))
        sCH = e(nc.semaphore("f0sCH"))
        sCA = e(nc.semaphore("f0sCA"))
        dO2 = [e(nc.semaphore(f"f0dO{i}")) for i in range(2)]

        block = e(nc.Block())

        @block.gpsimd
        def _(gp):
            xv = xin.rearrange("(c p) s -> p c s", p=128)
            gp.dma_start(out=xbf[:], in_=xv).then_inc(dX, 16)
            for sup in range(2):
                for jg in range(NG):
                    base = sup * NG + jg
                    if base > 0:
                        gp.wait_ge(sPE, 32 * base)
                    for j8 in range(NJG):
                        jc = jg * NJG + j8
                        jglob = base * NJG + j8
                        gp.dma_start(out=w1g[j8][:], in_=w1c[jc]).then_inc(dW1, 16)
                        if jglob >= 2:
                            gp.wait_ge(sD1, 2 * jglob - 2)
                        gp.dma_start(out=w0t[jglob % 2][:],
                                     in_=w0c[jc]).then_inc(dW0[jglob % 2], 16)

        @block.tensor
        def _(te):
            te.wait_ge(dX, 16 * 1)
            te.wait_ge(dC, 16 * 5)
            for sup in range(2):
                s0 = sup * SUP
                for jg in range(NG):
                    base = sup * NG + jg
                    for j8 in range(NJG):
                        jglob = base * NJG + j8
                        te.wait_ge(dW0[jglob % 2], 16 * (jglob // 2 + 1))
                        for sb in range(NSB):
                            i = jglob * NSB + sb
                            if i >= 2:
                                te.wait_ge(sD1, i - 1)
                            for hc in range(NDC):
                                ins = te.matmul(
                                    out=psD[i % 2][:],
                                    lhsT=w0t[jglob % 2][:, hc, :],
                                    rhs=xbf[:, hc, s0 + sb * 512: s0 + (sb + 1) * 512],
                                    start=(hc == 0), stop=(hc == NDC - 1))
                                if hc == NDC - 1:
                                    ins.then_inc(sPD, 1)
                    te.wait_ge(dW1, 16 * NJG * (base + 1))
                    te.wait_ge(sD1, (base * NJG + NJG) * NSB)
                    for hc in range(NDC):
                        for sb in range(NSB):
                            j = base * NDC * NSB + hc * NSB + sb
                            if j >= 2:
                                te.wait_ge(sEE, j - 1)
                            for j8 in range(NJG):
                                ins = te.matmul(
                                    out=psE[j % 2][:],
                                    lhsT=w1g[j8][:, hc * 128:(hc + 1) * 128],
                                    rhs=d1[:, j8, sb * 512:(sb + 1) * 512],
                                    start=(j8 == 0), stop=(j8 == NJG - 1))
                                if j8 == NJG - 1:
                                    ins.then_inc(sPE, 1)
                for sb in range(NSB):
                    k = sup * NSB + sb
                    te.wait_ge(sACC, k * NDC + NDC)
                    if k >= 1:
                        te.wait_ge(sRD, k)
                    for hc in range(NDC):
                        te.matmul(out=psSum[:], lhsT=onescs[:],
                                  rhs=eacc[:, hc, sb * 512:(sb + 1) * 512],
                                  start=(hc == 0), stop=(hc == NDC - 1))
                    for hc in range(NDC):
                        te.wait_ge(sSQ, k * NDC + hc + 1)
                        te.matmul(out=psSq[:], lhsT=onescs[:], rhs=sqt[hc % 2][:],
                                  start=(hc == 0), stop=(hc == NDC - 1)
                                  ).then_inc(sST, 1)
                    te.wait_ge(sRW, k + 1)
                    if k >= 1:
                        te.wait_ge(sNM, k * NDC)
                    te.matmul(out=psB1[:], lhsT=onesrs[:], rhs=rowr[:],
                              start=True, stop=True)
                    te.matmul(out=psB2[:], lhsT=onesrs[:], rhs=rowmr[:],
                              start=True, stop=True).then_inc(sBC, 1)

        @block.scalar
        def _(sc):
            sc.dma_start(out=b0s[:], in_=b0c).then_inc(dC, 16)
            sc.dma_start(out=b1s[:], in_=b1c).then_inc(dC, 16)
            sc.dma_start(out=onesrs[:], in_=onesr).then_inc(dC, 16)
            sc.dma_start(out=onescs[:], in_=onesc).then_inc(dC, 16)
            sc.dma_start(out=cmisc[:], in_=cmiscd).then_inc(dC, 16)
            sc.wait_ge(dC, 16 * 5)
            for sup in range(2):
                for jg in range(NG):
                    base = sup * NG + jg
                    for j8 in range(NJG):
                        jglob = base * NJG + j8
                        jc = jg * NJG + j8
                        for sb in range(NSB):
                            i = jglob * NSB + sb
                            sc.wait_ge(sPD, i + 1)
                            sc.activation(out=d1[:, j8, sb * 512:(sb + 1) * 512],
                                          in_=psD[i % 2][:], func=AF.Gelu,
                                          bias=b0s[:, jc:jc + 1]).then_inc(sD1, 1)
                for sb in range(NSB):
                    k = sup * NSB + sb
                    for hc in range(NDC):
                        gidx = k * NDC + hc
                        sc.wait_ge(sACC, gidx + 1)
                        if gidx >= 2:
                            sc.wait_ge(sST, gidx - 1)
                        sc.activation(out=sqt[hc % 2][:],
                                      in_=eacc[:, hc, sb * 512:(sb + 1) * 512],
                                      func=AF.Square).then_inc(sSQ, 1)
                    sc.wait_ge(sR1, k + 1)
                    sc.activation(out=rowr[:], in_=rowv[:], func=AF.Ln,
                                  bias=LN_EPS).then_inc(sCA, 1)
                    sc.wait_ge(sCA, k + 1)
                    sc.activation(out=rowr[:], in_=rowr[:], func=AF.Exp,
                                  scale=-0.5).then_inc(sR2, 1)

        @block.vector
        def _(ve):
            nch = 0

            def step(ins):
                nonlocal nch
                nch += 1
                ins.then_inc(sCH, 1)
                ve.wait_ge(sCH, nch)

            ve.wait_ge(dC, 16 * 5)
            for sup in range(2):
                s0 = sup * SUP
                for jg in range(NG):
                    base = sup * NG + jg
                    for hc in range(NDC):
                        for sb in range(NSB):
                            j = base * NDC * NSB + hc * NSB + sb
                            ve.wait_ge(sPE, j + 1)
                            dst = eacc[:, hc, sb * 512:(sb + 1) * 512]
                            if jg == 0:
                                ve.tensor_copy(out=dst, in_=psE[j % 2][:]).then_inc(sEE, 1)
                            else:
                                ve.tensor_add(out=dst, in0=dst, in1=psE[j % 2][:]).then_inc(sEE, 1)
                ve.wait_ge(sEE, 256 * (sup + 1))
                for sb in range(NSB):
                    k = sup * NSB + sb
                    for hc in range(NDC):
                        ve.scalar_tensor_tensor(
                            out=eacc[:, hc, sb * 512:(sb + 1) * 512],
                            in0=eacc[:, hc, sb * 512:(sb + 1) * 512],
                            scalar=b1s[:, hc:hc + 1],
                            in1=xbf[:, hc, s0 + sb * 512: s0 + (sb + 1) * 512],
                            op0=ALU.add, op1=ALU.add).then_inc(sACC, 1)
                    ve.wait_ge(sST, (k + 1) * NDC)
                    step(ve.tensor_scalar_mul(out=rowm[:], in0=psSum[:], scalar1=1.0 / D))
                    step(ve.tensor_scalar_mul(out=rowq[:], in0=psSq[:], scalar1=1.0 / D))
                    ve.nop().then_inc(sRD, 1)
                    step(ve.tensor_mul(out=rowv[:], in0=rowm[:], in1=rowm[:]))
                    ve.tensor_sub(out=rowv[:], in0=rowq[:], in1=rowv[:]).then_inc(sR1, 1)
                    ve.wait_ge(sR2, k + 1)
                    ve.tensor_mul(out=rowmr[:], in0=rowm[:], in1=rowr[:]).then_inc(sRW, 1)
                    ve.wait_ge(sBC, k + 1)
                    for hc in range(NDC):
                        n = k * NDC + hc
                        step(ve.tensor_mul(out=tnrm[:],
                                           in0=eacc[:, hc, sb * 512:(sb + 1) * 512],
                                           in1=psB1[:]))
                        ve.tensor_sub(out=xbf[:, hc, s0 + sb * 512: s0 + (sb + 1) * 512],
                                      in0=tnrm[:], in1=psB2[:]).then_inc(sNM, 1)
                        ve.wait_ge(sNM, n + 1)

        @block.sync
        def _(sy):
            sy.nop()


def emit_l1a(nc, E):
    xin = E["xin"]
    wdtind = E["wdtin"]
    wdtoutd = E["wdtout"]
    wBd = E["wB"]
    wCd = E["wC"]
    acolsd = E["acols"]
    dcolsd = E["dcols"]
    bdtd = E["bdtcols"]
    oh16d = E["c_oh16"]
    onesrd = E["c_ones_row"]
    onescd = E["c_ones_col"]
    cmiscd = E["c_misc"]
    xout = E["xout"]

    TC = 512
    NTC = SQ // TC

    with ExitStack() as st:
        e = st.enter_context
        xbf = E["xshared"]
        dt1b = e(nc.sbuf_tensor("s1dt1b", [DTR, SQ], BF16))
        wdtinb = e(nc.sbuf_tensor("s1wdtinb", [128, NDC, DTR], BF16))
        wdtoutb = e(nc.sbuf_tensor("s1wdtoutb", [DTR, D], BF16))
        wBb = e(nc.sbuf_tensor("s1wBb", [128, NDC, STN], BF16))
        wCb = e(nc.sbuf_tensor("s1wCb", [128, NDC, STN], BF16))
        oh16b = e(nc.sbuf_tensor("s1oh16b", [STN, STN * 128], BF16))
        onesrs = e(nc.sbuf_tensor("s1onesrs", [1, 128], F32))
        onescb = e(nc.sbuf_tensor("s1onescb", [128, 1], BF16))
        acolsf = e(nc.sbuf_tensor("s1acolsf", [128, NDC, STN], F32))
        aneg = acolsf[:].rearrange("p c n -> p (c n)")
        dcolsf = e(nc.sbuf_tensor("s1dcolsf", [128, NDC], F32))
        dp1 = e(nc.sbuf_tensor("s1dp1", [128, NDC], F32))
        bdtf = e(nc.sbuf_tensor("s1bdtf", [128, NDC], F32))
        Bsb = e(nc.sbuf_tensor("s1Bsb", [STN, TC], BF16))
        Csb = e(nc.sbuf_tensor("s1Csb", [STN, TC], BF16))
        Bb = e(nc.sbuf_tensor("s1Bb", [128, STN, TC], BF16))
        CBb = e(nc.sbuf_tensor("s1CBb", [128, STN, TC], BF16))
        dtc = [e(nc.sbuf_tensor(f"s1dtc{i}", [128, TC], F32)) for i in range(2)]
        abuf = [e(nc.sbuf_tensor(f"s1abuf{i}", [128, STN, TC], BF16)) for i in range(2)]
        dtu = [e(nc.sbuf_tensor(f"s1dtu{i}", [128, TC], BF16)) for i in range(2)]
        dtuB = e(nc.sbuf_tensor("s1dtuB", [128, STN, TC], BF16))
        ytmp = e(nc.sbuf_tensor("s1ytmp", [128, TC], F32))
        s0 = e(nc.sbuf_tensor("s1s0", [128, NDC * STN], F32))
        accb = e(nc.sbuf_tensor("s1accb", [128, NDC, TC], BF16))
        t1t = e(nc.sbuf_tensor("s1t1t", [128, TC], F32))
        t2t = e(nc.sbuf_tensor("s1t2t", [128, TC], F32))
        t4t = e(nc.sbuf_tensor("s1t4t", [128, TC], F32))
        sqL = [e(nc.sbuf_tensor(f"s1sqL{i}", [128, TC], BF16)) for i in range(2)]
        x3t = [e(nc.sbuf_tensor(f"s1x3t{i}", [128, TC], F32)) for i in range(2)]
        rowm = e(nc.sbuf_tensor("s1rowm", [1, TC], F32))
        rowq = e(nc.sbuf_tensor("s1rowq", [1, TC], F32))
        rowv = e(nc.sbuf_tensor("s1rowv", [1, TC], F32))
        rowr = e(nc.sbuf_tensor("s1rowr", [1, TC], F32))
        cmisc = e(nc.sbuf_tensor("s1cmisc", [128, len(MISC_VALS)], F32))
        _reg_misc(nc, cmisc)

        ps = [e(nc.psum_tensor(f"s1ps{i}", [128, 512], F32)) for i in range(8)]
        psDt = [ps[0], ps[1]]
        psBC = ps[2]
        psBr = [ps[3], ps[4]]
        psLNr = ps[7][:, 0:TC]
        psLNm = ps[6][:, 0:TC]
        psSum = ps[5][0:1, 0:TC]
        psSq = ps[6][0:1, 0:TC]

        dX = e(nc.semaphore("s1dX"))
        dC = e(nc.semaphore("s1dC"))
        dW = e(nc.semaphore("s1dW"))
        sPR = e(nc.semaphore("s1sPR"))
        sD1 = e(nc.semaphore("s1sD1"))
        sD1E = e(nc.semaphore("s1sD1E"))
        sPB = e(nc.semaphore("s1sPB"))
        sBCE = e(nc.semaphore("s1sBCE"))
        sBB = e(nc.semaphore("s1sBB"))
        sBBE = e(nc.semaphore("s1sBBE"))
        sPD = e(nc.semaphore("s1sPD"))
        sSP = e(nc.semaphore("s1sSP"))
        sSPF = e(nc.semaphore("s1sSPF"))
        sDT = e(nc.semaphore("s1sDT"))
        sAG = e(nc.semaphore("s1sAG"))
        sSN = e(nc.semaphore("s1sSN"))
        sSCM = e(nc.semaphore("s1sSCM"))
        sACC = e(nc.semaphore("s1sACC"))
        sSQ = e(nc.semaphore("s1sSQ"))
        sST = e(nc.semaphore("s1sST"))
        sRD = e(nc.semaphore("s1sRD"))
        sR1 = e(nc.semaphore("s1sR1"))
        sR2 = e(nc.semaphore("s1sR2"))
        sRW = e(nc.semaphore("s1sRW"))
        sBCst = e(nc.semaphore("s1sBCst"))
        sNM = e(nc.semaphore("s1sNM"))
        sCH = e(nc.semaphore("s1sCH"))
        sCA = e(nc.semaphore("s1sCA"))
        dO2 = [e(nc.semaphore(f"s1dO{i}")) for i in range(2)]

        block = e(nc.Block())

        @block.gpsimd
        def _(gp):

            gp.dma_start(out=wdtinb[:], in_=wdtind).then_inc(dW, 16)
            gp.dma_start(out=wdtoutb[:], in_=wdtoutd).then_inc(dW, 16)
            gp.dma_start(out=wBb[:], in_=wBd).then_inc(dW, 16)
            gp.dma_start(out=wCb[:], in_=wCd).then_inc(dW, 16)
            gp.dma_start(out=oh16b[:], in_=oh16d).then_inc(dW, 16)
            gp.dma_start(out=onescb[:], in_=onescd).then_inc(dW, 16)

        NW = 6

        @block.tensor
        def _(te):
            te.wait_ge(dW, 16 * NW)
            te.wait_ge(dC, 16 * 5)
            for sb in range(4):
                if sb >= 2:
                    te.wait_ge(sD1E, sb - 1)
                for hc in range(NDC):
                    ins = te.matmul(out=psDt[sb % 2][0:DTR, :],
                                    lhsT=wdtinb[:, hc, :],
                                    rhs=xbf[:, hc, sb * 512:(sb + 1) * 512],
                                    start=(hc == 0), stop=(hc == NDC - 1))
                    if hc == NDC - 1:
                        ins.then_inc(sD1, 1)
            def pe_ln(tc):
                te.wait_ge(sACC, 16 * (tc + 1))
                if tc > 0:
                    te.wait_ge(sRD, tc)
                for hc in range(NDC):
                    te.matmul(out=psSum, lhsT=onescb[:], rhs=accb[:, hc, :],
                              start=(hc == 0), stop=(hc == NDC - 1))
                if tc > 0:
                    te.wait_ge(sNM, 16 * tc)
                for hc in range(NDC):
                    te.wait_ge(sSQ, tc * NDC + hc + 1)
                    te.matmul(out=psSq, lhsT=onescb[:], rhs=sqL[hc % 2][:],
                              start=(hc == 0), stop=(hc == NDC - 1)).then_inc(sST, 1)
                te.wait_ge(sRW, tc + 1)
                te.matmul(out=psLNr, lhsT=onesrs[:], rhs=rowr[0:1, :],
                          start=True, stop=True)
                te.matmul(out=psLNm, lhsT=onesrs[:], rhs=rowq[0:1, :],
                          start=True, stop=True).then_inc(sBCst, 1)

            for tc in range(NTC):
                t0 = tc * TC
                for bc in range(2):
                    te.wait_ge(sBCE, 2 * tc + bc)
                    w = wBb if bc == 0 else wCb
                    for hc in range(NDC):
                        ins = te.matmul(out=psBC[0:STN, :],
                                        lhsT=w[:, hc, :],
                                        rhs=xbf[:, hc, t0:t0 + TC],
                                        start=(hc == 0), stop=(hc == NDC - 1))
                        if hc == NDC - 1:
                            ins.then_inc(sPB, 1)
                for i in range(2 * STN):
                    gb = tc * 2 * STN + i
                    te.wait_ge(sBCE, 2 * tc + (1 if i < STN else 2))
                    if gb >= 2:
                        te.wait_ge(sBBE, gb - 1)
                    src = Bsb if i < STN else Csb
                    n = i % STN
                    te.matmul(out=psBr[gb % 2][:],
                              lhsT=oh16b[0:STN, n * 128:(n + 1) * 128],
                              rhs=src[0:STN, :],
                              start=True, stop=True).then_inc(sBB, 1)
                if tc == 0:
                    te.wait_ge(sD1E, 4)
                for hc in range(NDC):
                    ghc = tc * NDC + hc
                    if ghc >= 2:
                        te.wait_ge(sSP, ghc - 1)
                    te.matmul(out=psDt[ghc % 2][:],
                              lhsT=wdtoutb[0:DTR, hc * 128:(hc + 1) * 128],
                              rhs=dt1b[0:DTR, t0:t0 + TC],
                              start=True, stop=True).then_inc(sPD, 1)
                    if hc == 0 and tc > 0:
                        pe_ln(tc - 1)
            pe_ln(NTC - 1)

        @block.scalar
        def _(sc):
            sc.dma_start(out=acolsf[:], in_=acolsd).then_inc(dC, 16)
            sc.dma_start(out=dcolsf[:], in_=dcolsd).then_inc(dC, 16)
            sc.dma_start(out=bdtf[:], in_=bdtd).then_inc(dC, 16)
            sc.dma_start(out=onesrs[:], in_=onesrd).then_inc(dC, 16)
            sc.dma_start(out=cmisc[:], in_=cmiscd).then_inc(dC, 16)
            sc.wait_ge(dC, 16 * 5)
            nca = 0

            def astep(ins):
                nonlocal nca
                nca += 1
                ins.then_inc(sCA, 1)
                sc.wait_ge(sCA, nca)

            astep(sc.activation(out=aneg, in_=aneg, func=AF.Exp))
            for sb in range(4):
                sc.wait_ge(sD1, sb + 1)
                sc.activation(out=dt1b[0:DTR, sb * 512:(sb + 1) * 512],
                              in_=psDt[sb % 2][0:DTR, :], func=AF.Copy).then_inc(sD1E, 1)
            for tc in range(NTC):
                t0 = tc * TC
                for bc in range(2):
                    sc.wait_ge(sPB, 2 * tc + bc + 1)
                    if tc > 0:
                        sc.wait_ge(sBB, 32 * tc)
                    dst = Bsb if bc == 0 else Csb
                    sc.activation(out=dst[0:STN, :], in_=psBC[0:STN, :],
                                  func=AF.Copy).then_inc(sBCE, 1)
                if tc > 0:
                    sc.wait_ge(sSCM, NDC * tc)
                for i in range(2 * STN):
                    gb = tc * 2 * STN + i
                    sc.wait_ge(sBB, gb + 1)
                    n = i % STN
                    dst = Bb if i < STN else CBb
                    sc.activation(out=dst[:, n, :], in_=psBr[gb % 2][:],
                                  func=AF.Copy).then_inc(sBBE, 1)
                for hc in range(NDC):
                    ghc = tc * NDC + hc
                    sc.wait_ge(sPD, ghc + 1)
                    if ghc >= 1:
                        sc.wait_ge(sDT, ghc)
                    astep(sc.activation(out=t1t[:], in_=psDt[ghc % 2][:],
                                        func=AF.Abs, bias=bdtf[:, hc:hc + 1]))
                    sc.activation(out=t4t[:], in_=psDt[ghc % 2][:],
                                  func=AF.Relu, bias=bdtf[:, hc:hc + 1]).then_inc(sSP, 1)
                    astep(sc.activation(out=t2t[:], in_=t1t[:], func=AF.Exp,
                                        scale=-1.0))
                    sc.activation(out=t1t[:], in_=t2t[:], func=AF.Ln,
                                  bias=1.0).then_inc(sSPF, 1)
                    if hc == 0 and tc > 0:
                        sc.wait_ge(sR1, tc)
                        astep(sc.activation(out=rowr[0:1, :], in_=rowv[0:1, :],
                                            func=AF.Ln, bias=LN_EPS))
                        sc.activation(out=rowr[0:1, :], in_=rowr[0:1, :], func=AF.Exp,
                                      scale=-0.5).then_inc(sR2, 1)
                    sc.wait_ge(sPR, 1)
                    sc.wait_ge(sDT, ghc + 1)
                    if ghc >= 2:
                        sc.wait_ge(sSCM, ghc - 1)
                    for n in range(STN):
                        ins = sc.activation(out=abuf[ghc % 2][:, n, :],
                                            in_=dtc[ghc % 2][:], func=AF.Exp,
                                            scale=aneg[:, hc * STN + n: hc * STN + n + 1])
                        if n == STN - 1:
                            ins.then_inc(sAG, 1)
            sc.wait_ge(sR1, NTC)
            astep(sc.activation(out=rowr[0:1, :], in_=rowv[0:1, :], func=AF.Ln,
                                bias=LN_EPS))
            sc.activation(out=rowr[0:1, :], in_=rowr[0:1, :], func=AF.Exp,
                          scale=-0.5).then_inc(sR2, 1)

        @block.vector
        def _(ve):
            nch = 0

            def step(ins):
                nonlocal nch
                nch += 1
                ins.then_inc(sCH, 1)
                ve.wait_ge(sCH, nch)

            ve.wait_ge(dC, 16 * 5)
            ve.wait_ge(dW, 16 * NW)
            ve.wait_ge(sCA, 1)
            ve.tensor_scalar_mul(out=aneg, in0=aneg, scalar1=-1.0).then_inc(sPR, 1)
            ve.tensor_scalar_add(out=dp1[:], in0=dcolsf[:], scalar1=1.0).then_inc(sPR, 1)
            ve.wait_ge(sPR, 2)
            ve.memset(s0[:], 0.0).then_inc(sPR, 1)
            NGH = NTC * NDC

            def ln_chunk(tc):
                t0 = tc * TC
                for hc in range(NDC):
                    g = tc * NDC + hc
                    if g >= 2:
                        ve.wait_ge(sST, g - 1)
                    ve.tensor_mul(out=sqL[hc % 2][:], in0=accb[:, hc, :],
                                  in1=accb[:, hc, :]).then_inc(sSQ, 1)
                ve.wait_ge(sST, (tc + 1) * NDC)
                step(ve.tensor_scalar_mul(out=rowm[0:1, :], in0=psSum, scalar1=1.0 / D))
                step(ve.tensor_scalar_mul(out=rowq[0:1, :], in0=psSq, scalar1=1.0 / D))
                ve.nop().then_inc(sRD, 1)
                step(ve.tensor_mul(out=rowv[0:1, :], in0=rowm[0:1, :], in1=rowm[0:1, :]))
                ve.tensor_sub(out=rowv[0:1, :], in0=rowq[0:1, :],
                              in1=rowv[0:1, :]).then_inc(sR1, 1)
                ve.wait_ge(sR2, tc + 1)
                ve.tensor_mul(out=rowq[0:1, :], in0=rowm[0:1, :],
                              in1=rowr[0:1, :]).then_inc(sRW, 1)
                ve.wait_ge(sBCst, tc + 1)
                for hc in range(NDC):
                    nn = tc * NDC + hc
                    step(ve.tensor_mul(out=x3t[hc % 2][:], in0=accb[:, hc, :],
                                       in1=psLNr))
                    ve.tensor_sub(out=xbf[:, hc, tc * TC:(tc + 1) * TC],
                                  in0=x3t[hc % 2][:],
                                  in1=psLNm).then_inc(sNM, 1)
                    ve.wait_ge(sNM, nn + 1)

            def front(k):
                tck, hck = divmod(k, NDC)
                t0k = tck * TC
                ve.wait_ge(sSPF, k + 1)
                if k >= 2:
                    ve.wait_ge(sAG, k - 1)
                step(ve.tensor_add(out=dtc[k % 2][:], in0=t1t[:], in1=t4t[:]))
                ve.nop().then_inc(sDT, 1)
                step(ve.tensor_mul(out=dtu[k % 2][:], in0=dtc[k % 2][:],
                                   in1=xbf[:, hck, t0k:t0k + TC]))

            for k in range(NGH + 1):
                boundary = (k % NDC == 0)
                if k < NGH and not boundary:
                    front(k)
                if k >= 1:
                    j = k - 1
                    tcj, hcj = divmod(j, NDC)
                    t0j = tcj * TC
                    ve.wait_ge(sAG, j + 1)
                    for n in range(STN):
                        ins = ve.tensor_tensor_scan(
                            out=abuf[j % 2][:, n, :],
                            data0=abuf[j % 2][:, n, :],
                            data1=dtuB[:, n, :],
                            initial=s0[:, hcj * STN + n: hcj * STN + n + 1],
                            op0=ALU.mult, op1=ALU.add)
                        if n < STN - 1:
                            ins.then_inc(sSN, 1)
                        else:
                            nch += 1
                            ins.then_inc(sCH, 1)
                            ve.wait_ge(sCH, nch)
                            ve.nop().then_inc(sSN, 1)
                    step(ve.tensor_copy(
                        out=s0[:, hcj * STN:(hcj + 1) * STN],
                        in_=abuf[j % 2][:, :, TC - 1:TC].rearrange("p n o -> p (n o)")))
                    ve.tensor_mul(out=dtuB[:], in0=abuf[j % 2][:],
                                  in1=CBb[:]).then_inc(sSCM, 1)
                    ve.wait_ge(sSCM, j + 1)
                    step(ve.tensor_add(out=dtuB[:, 0:8, :], in0=dtuB[:, 0:8, :],
                                       in1=dtuB[:, 8:16, :]))
                    step(ve.tensor_add(out=dtuB[:, 0:4, :], in0=dtuB[:, 0:4, :],
                                       in1=dtuB[:, 4:8, :]))
                    step(ve.tensor_add(out=dtuB[:, 0:2, :], in0=dtuB[:, 0:2, :],
                                       in1=dtuB[:, 2:4, :]))
                    step(ve.tensor_add(out=ytmp[:], in0=dtuB[:, 0, :],
                                       in1=dtuB[:, 1, :]))
                    step(ve.scalar_tensor_tensor(out=accb[:, hcj, :],
                                                 in0=xbf[:, hcj, t0j:t0j + TC],
                                                 scalar=dp1[:, hcj:hcj + 1],
                                                 in1=ytmp[:],
                                                 op0=ALU.mult, op1=ALU.add))
                    ve.nop().then_inc(sACC, 1)
                    if hcj == NDC - 1:
                        ln_chunk(tcj)
                if k < NGH:
                    if boundary:
                        front(k)
                    bcast_dtu = dtu[k % 2][:].rearrange("p (a t) -> p a t", a=1).broadcast_to([128, STN, TC])
                    step(ve.tensor_mul(out=dtuB[:], in0=Bb[:], in1=bcast_dtu))

        @block.sync
        def _(sy):
            sy.nop()


def emit_moe(nc, E):
    xin = E["xin"]
    wec = E["wecube"]
    gcd = E["gcube"]
    gbd = E["gb_row"]
    ebd = E["ebcols"]
    oh8d = E["c_oh8"]
    onesrd = E["c_ones_row"]
    idd = E["c_id"]
    cmiscd = E["c_misc"]
    out = E["xout"]

    SUP = 1024
    NSC = SQ // 128

    with ExitStack() as st:
        e = st.enter_context
        xbf = E["xshared"]
        sel = e(nc.sbuf_tensor("m1sel", [128, NDC, SUP], BF16))
        wet = [e(nc.sbuf_tensor(f"m1wet{i}", [128, D], BF16)) for i in range(2)]
        gcs = e(nc.sbuf_tensor("m1gcs", [128, NDC, NE], BF16))
        gbs = e(nc.sbuf_tensor("m1gbs", [1, NE], BF16))
        onesrb = e(nc.sbuf_tensor("m1onesrb", [1, 128], BF16))
        oh8b = e(nc.sbuf_tensor("m1oh8b", [NE, NE * 128], BF16))
        idb = e(nc.sbuf_tensor("m1idb", [128, 128], BF16))
        ids = e(nc.sbuf_tensor("m1ids", [128, 128], F32))
        ebs = e(nc.sbuf_tensor("m1ebs", [128, NE, NDC], F32))
        cmisc = e(nc.sbuf_tensor("m1cmisc", [128, len(MISC_VALS)], F32))
        _reg_misc(nc, cmisc)
        lg = e(nc.sbuf_tensor("m1lg", [128, NE], F32))
        m1 = e(nc.sbuf_tensor("m1m1", [128, 1], F32))
        eq1 = e(nc.sbuf_tensor("m1eq1", [128, NE], F32))
        neg = e(nc.sbuf_tensor("m1neg", [128, NE], F32))
        m2 = e(nc.sbuf_tensor("m1m2", [128, 1], F32))
        eq2 = e(nc.sbuf_tensor("m1eq2", [128, NE], F32))
        wsum = e(nc.sbuf_tensor("m1wsum", [128, NE], F32))
        wbfT = e(nc.sbuf_tensor("m1wbfT", [128, NE], BF16))
        wT = e(nc.sbuf_tensor("m1wT", [NE, SQ], BF16))
        wbs = [e(nc.sbuf_tensor(f"m1wbs{i}", [128, 512], BF16)) for i in range(2)]
        Rt = [e(nc.sbuf_tensor(f"m1Rt{i}", [128, 512], BF16)) for i in range(2)]
        mtmp = e(nc.sbuf_tensor("m1mtmp", [128, 512], BF16))
        accf = [e(nc.sbuf_tensor(f"m1accf{i}", [128, 128], F32)) for i in range(2)]
        otok = [e(nc.sbuf_tensor(f"m1otok{i}", [128, D], F32)) for i in range(2)]
        sqf = [e(nc.sbuf_tensor(f"m1sqf{i}", [128, D], F32)) for i in range(2)]
        ofin = [e(nc.sbuf_tensor(f"m1ofin{i}", [128, D], F32)) for i in range(2)]
        scol = e(nc.sbuf_tensor("m1scol", [128, 1], F32))
        qcol = e(nc.sbuf_tensor("m1qcol", [128, 1], F32))
        mcol = e(nc.sbuf_tensor("m1mcol", [128, 1], F32))
        vcol = e(nc.sbuf_tensor("m1vcol", [128, 1], F32))
        rcol = e(nc.sbuf_tensor("m1rcol", [128, 1], F32))
        tcol = e(nc.sbuf_tensor("m1tcol", [128, 1], F32))

        psX = [e(nc.psum_tensor(f"m1psX{i}", [128, 512], F32)) for i in range(4)]
        psW = e(nc.psum_tensor("m1psW", [128, 512], F32))
        psG = e(nc.psum_tensor("m1psG", [128, 512], F32))
        psTw = e(nc.psum_tensor("m1psTw", [128, 256], BF16))
        psT = [psG[:, 0:128], psTw[:].bitcast(F32)[:, 0:128]]

        dX = e(nc.semaphore("m1dX"))
        dC = e(nc.semaphore("m1dC"))
        dWE = [e(nc.semaphore(f"m1dWE{i}")) for i in range(2)]
        sPG = e(nc.semaphore("m1sPG"))
        sLG = e(nc.semaphore("m1sLG"))
        sW1 = e(nc.semaphore("m1sW1"))
        sTW = e(nc.semaphore("m1sTW"))
        sW2 = e(nc.semaphore("m1sW2"))
        sWB = e(nc.semaphore("m1sWB"))
        sWBE = e(nc.semaphore("m1sWBE"))
        sXG = e(nc.semaphore("m1sXG"))
        sRL = e(nc.semaphore("m1sRL"))
        sSEL = e(nc.semaphore("m1sSEL"))
        sA4 = e(nc.semaphore("m1sA4"))
        sT4 = e(nc.semaphore("m1sT4"))
        sE4 = e(nc.semaphore("m1sE4"))
        sS4 = e(nc.semaphore("m1sS4"))
        sC4 = e(nc.semaphore("m1sC4"))
        sNM = e(nc.semaphore("m1sNM"))
        sCH = e(nc.semaphore("m1sCH"))
        sCA = e(nc.semaphore("m1sCA"))
        dO2 = [e(nc.semaphore(f"m1dO{i}")) for i in range(2)]

        block = e(nc.Block())

        @block.gpsimd
        def _(gp):

            gp.dma_start(out=gcs[:], in_=gcd).then_inc(dX, 16)
            gp.dma_start(out=gbs[:], in_=gbd).then_inc(dX, 16)
            gp.dma_start(out=onesrb[:], in_=onesrd).then_inc(dX, 16)
            gp.dma_start(out=oh8b[:], in_=oh8d).then_inc(dX, 16)
            gp.dma_start(out=idb[:], in_=idd).then_inc(dX, 16)
            for sup in range(2):
                for ei in range(NE):
                    for oc in range(NDC):
                        idx = (sup * NE + ei) * NDC + oc
                        if idx >= 2:
                            gp.wait_ge(sXG, 2 * (idx - 1))
                        gp.dma_start(out=wet[idx % 2][:],
                                     in_=wec[ei, oc]).then_inc(dWE[idx % 2], 16)

        @block.tensor
        def _(te):
            te.wait_ge(dX, 16 * 5)
            te.wait_ge(dC, 16 * 3)
            for sc in range(NSC):
                if sc >= 1:
                    te.wait_ge(sLG, sc)
                    te.wait_ge(sW2, sc)
                for dc in range(NDC):
                    te.matmul(out=psG[:, 0:NE], lhsT=xbf[:, dc, sc * 128:(sc + 1) * 128],
                              rhs=gcs[:, dc, :], start=(dc == 0), stop=False)
                te.matmul(out=psG[:, 0:NE], lhsT=onesrb[:], rhs=gbs[:],
                          start=False, stop=True).then_inc(sPG, 1)
                te.wait_ge(sW1, sc + 1)
                te.transpose(out=psTw[0:NE, 0:128], in_=wbfT[:],
                             identity=idb[:]).then_inc(sTW, 1)
            for sup in range(2):
                s0 = sup * SUP
                for ei in range(NE):
                    eg = sup * NE + ei
                    te.wait_ge(sW2, NSC)
                    for sb in range(2):
                        if eg > 0 or sb > 0:
                            te.wait_ge(sWBE, 2 * eg + sb)
                        te.matmul(out=psW[:], lhsT=oh8b[0:NE, ei * 128:(ei + 1) * 128],
                                  rhs=wT[0:NE, s0 + sb * 512: s0 + (sb + 1) * 512],
                                  start=True, stop=True).then_inc(sWB, 1)
                    for oc in range(NDC):
                        idx = eg * NDC + oc
                        te.wait_ge(dWE[idx % 2], 16 * (idx // 2 + 1))
                        for sb in range(2):
                            j = idx * 2 + sb
                            if j >= 4:
                                te.wait_ge(sRL, j - 3)
                            for dc in range(NDC):
                                ins = te.matmul(
                                    out=psX[j % 4][:],
                                    lhsT=wet[idx % 2][:, dc * 128:(dc + 1) * 128],
                                    rhs=xbf[:, dc, s0 + sb * 512: s0 + (sb + 1) * 512],
                                    start=(dc == 0), stop=(dc == NDC - 1))
                                if dc == NDC - 1:
                                    ins.then_inc(sXG, 1)
                for sc8 in range(8):
                    sc = sup * 8 + sc8
                    for oc in range(NDC):
                        ti = sc * NDC + oc
                        te.wait_ge(sA4, ti + 1)
                        if ti >= 2:
                            te.wait_ge(sE4, ti - 1)
                        te.transpose(out=psT[ti % 2], in_=accf[ti % 2][:],
                                     identity=ids[:]).then_inc(sT4, 1)

        @block.scalar
        def _(sc_):
            nca = 0

            def astep(ins):
                nonlocal nca
                nca += 1
                ins.then_inc(sCA, 1)
                sc_.wait_ge(sCA, nca)

            sc_.dma_start(out=ebs[:], in_=ebd).then_inc(dC, 16)
            sc_.dma_start(out=ids[:], in_=idd).then_inc(dC, 16)
            sc_.dma_start(out=cmisc[:], in_=cmiscd).then_inc(dC, 16)
            sc_.wait_ge(dC, 16 * 3)
            for sup in range(2):
                for ei in range(NE):
                    eg = sup * NE + ei
                    for sb in range(2):
                        sc_.wait_ge(sWB, 2 * eg + sb + 1)
                        if eg > 0:
                            sc_.wait_ge(sSEL, 32 * eg + sb - 1)
                        sc_.activation(out=wbs[sb][:], in_=psW[:],
                                       func=AF.Copy).then_inc(sWBE, 1)
                    for oc in range(NDC):
                        idx = eg * NDC + oc
                        for sb in range(2):
                            j = idx * 2 + sb
                            sc_.wait_ge(sXG, j + 1)
                            if j >= 2:
                                sc_.wait_ge(sSEL, j - 1)
                            sc_.activation(out=Rt[j % 2][:], in_=psX[j % 4][:],
                                           func=AF.Relu,
                                           bias=ebs[:, ei, oc:oc + 1]).then_inc(sRL, 1)
                for sc8 in range(8):
                    sc = sup * 8 + sc8
                    par = sc % 2
                    for oc in range(NDC):
                        ti = sc * NDC + oc
                        sc_.wait_ge(sT4, ti + 1)
                        sc_.copy(out=otok[par][:, oc * 128:(oc + 1) * 128],
                                 in_=psT[ti % 2]).then_inc(sE4, 1)
                    sc_.wait_ge(sE4, 16 * (sc + 1))
                    astep(sc_.activation(out=sqf[par][:], in_=otok[par][:],
                                         func=AF.Copy, accum_out=scol[:]))
                    sc_.activation(out=sqf[par][:], in_=otok[par][:], func=AF.Square,
                                   accum_out=qcol[:]).then_inc(sS4, 1)
                    sc_.wait_ge(sC4, 2 * sc + 1)
                    astep(sc_.activation(out=rcol[:], in_=vcol[:], func=AF.Ln,
                                         bias=LN_EPS))
                    sc_.activation(out=rcol[:], in_=rcol[:], func=AF.Exp,
                                   scale=-0.5).then_inc(sC4, 1)

        @block.vector
        def _(ve):
            ve.wait_ge(dX, 16 * 5)
            nch = 0

            def step(ins):
                nonlocal nch
                nch += 1
                ins.then_inc(sCH, 1)
                ve.wait_ge(sCH, nch)

            for sc in range(NSC):
                ve.wait_ge(sPG, sc + 1)
                step(ve.tensor_copy(out=lg[:], in_=psG[:, 0:NE]))
                ve.nop().then_inc(sLG, 1)
                step(ve.reduce_max(out=m1[:], in_=lg[:], axis=mybir.AxisListType.X))
                step(ve.tensor_single_scalar(out=eq1[:], in_=lg[:], scalar=m1[:], op=ALU.is_ge))
                step(ve.scalar_tensor_tensor(out=neg[:], in0=eq1[:], scalar=-1e30,
                                             in1=lg[:], op0=ALU.mult, op1=ALU.add))
                step(ve.reduce_max(out=m2[:], in_=neg[:], axis=mybir.AxisListType.X))
                step(ve.tensor_single_scalar(out=eq2[:], in_=neg[:], scalar=m2[:], op=ALU.is_ge))
                step(ve.tensor_add(out=wsum[:], in0=eq1[:], in1=eq2[:]))
                ve.tensor_scalar_mul(out=wbfT[:], in0=wsum[:], scalar1=0.5).then_inc(sW1, 1)
                ve.wait_ge(sTW, sc + 1)
                ve.tensor_copy(out=wT[0:NE, sc * 128:(sc + 1) * 128],
                               in_=psTw[0:NE, 0:128]).then_inc(sW2, 1)
            for sup in range(2):
                s0 = sup * SUP
                for ei in range(NE):
                    eg = sup * NE + ei
                    for oc in range(NDC):
                        idx = eg * NDC + oc
                        for sb in range(2):
                            j = idx * 2 + sb
                            ve.wait_ge(sRL, j + 1)
                            ve.wait_ge(sWBE, 2 * eg + sb + 1)
                            dst = sel[:, oc, sb * 512:(sb + 1) * 512]
                            if ei == 0:
                                ve.tensor_mul(out=dst, in0=Rt[j % 2][:],
                                              in1=wbs[sb][:]).then_inc(sSEL, 1)
                            else:
                                step(ve.tensor_mul(out=mtmp[:], in0=Rt[j % 2][:],
                                                   in1=wbs[sb][:]))
                                ve.tensor_add(out=dst, in0=dst,
                                              in1=mtmp[:]).then_inc(sSEL, 1)
                                ve.wait_ge(sSEL, j + 1)
                for sc8 in range(8):
                    sc = sup * 8 + sc8
                    par = sc % 2
                    loc = sc8 * 128
                    for oc in range(NDC):
                        ti = sc * NDC + oc
                        if ti >= 2:
                            ve.wait_ge(sT4, ti - 1)
                        ve.tensor_add(out=accf[ti % 2][:],
                                      in0=xbf[:, oc, sc * 128:(sc + 1) * 128],
                                      in1=sel[:, oc, loc:loc + 128]).then_inc(sA4, 1)
                    ve.wait_ge(sS4, sc + 1)
                    step(ve.tensor_scalar_mul(out=mcol[:], in0=scol[:], scalar1=1.0 / D))
                    step(ve.tensor_scalar_mul(out=qcol[:], in0=qcol[:], scalar1=1.0 / D))
                    step(ve.tensor_mul(out=tcol[:], in0=mcol[:], in1=mcol[:]))
                    ve.tensor_sub(out=vcol[:], in0=qcol[:], in1=tcol[:]).then_inc(sC4, 1)
                    ve.wait_ge(sC4, 2 * sc + 2)
                    if sc >= 2:
                        ve.wait_ge(dO2[sc % 2], 16 * (sc // 2))
                    ve.tensor_scalar(out=ofin[par][:], in0=otok[par][:],
                                     scalar1=mcol[:], scalar2=rcol[:],
                                     op0=ALU.subtract, op1=ALU.mult).then_inc(sNM, 1)

        @block.sync
        def _(sy):
            for sc in range(NSC):
                sy.wait_ge(sNM, sc + 1)
                sy.dma_start(out=out[sc * 128:(sc + 1) * 128, :],
                             in_=ofin[sc % 2][:]).then_inc(dO2[sc % 2], 16)
            sy.wait_ge(dO2[0], 16 * (NSC // 2))
            sy.wait_ge(dO2[1], 16 * (NSC // 2))



_EXT_SPECS = {
    "xin": [D, SQ], "wq": [NDC, 128, D], "wk": [NDC, 128, D], "wv": [NDC, 128, D],
    "wo": [NDC, 128, D], "bq": [128, NDC], "bk": [128, NDC], "bo": [128, NDC],
    "bv_row": [1, D], "w0cube": [NJC, 128, NDC, 128], "w1cube": [NJC, 128, D],
    "b0cols": [128, NJC], "b1cols": [128, NDC],
    "wdtin": [128, NDC, DTR], "wdtout": [DTR, D], "wB": [128, NDC, STN],
    "wC": [128, NDC, STN], "acols": [128, NDC, STN], "dcols": [128, NDC],
    "bdtcols": [128, NDC], "wecube": [NE, NDC, 128, D], "gcube": [128, NDC, NE],
    "gb_row": [1, NE], "ebcols": [128, NE, NDC],
    "c_tri": [128, 128], "c_id": [128, 128], "c_ones_row": [1, 128],
    "c_ones_col": [128, 1], "c_oh8": [NE, NE * 128], "c_oh16": [STN, STN * 128],
    "c_misc": [128, len(MISC_VALS)],
}


def build_fused():
    nc = bass.Bass("TRN2", target_bir_lowering=False)
    ext = {k: nc.dram_tensor(k, shp, F32, kind="ExternalInput").ap()
           for k, shp in _EXT_SPECS.items()}
    x1T = nc.dram_tensor("x1T", [D, SQ], BF16, kind="Internal").ap()
    outd = nc.dram_tensor("out", [SQ, D], F32, kind="ExternalOutput").ap()

    E = dict(ext)
    E.update(xin=ext["xin"], xout=x1T)
    with nc.reset_on_exit():
        emit_l0a(nc, E)
    xsh_guard = nc.sbuf_tensor("xshared", [128, NDC, SQ], BF16)
    xshared = xsh_guard.__enter__()
    for emit, io in (
        (emit_l0f, dict(xin=x1T, xout=outd)),
        (emit_l1a, dict(xin=x1T, xout=outd)),
        (emit_moe, dict(xin=x1T, xout=outd)),
    ):
        E = dict(ext)
        E.update(io)
        E["xshared"] = xshared
        with nc.reset_on_exit():
            emit(nc, E)
    return nc


def kernel(**inputs):
    g = prep_host(inputs)
    nc = build_fused()
    shared = {k: g[k] for k in _EXT_SPECS if k != "xin"}
    in_maps = [dict(shared, xin=g["xT"][b]) for b in range(NB)]
    res = run_bass_kernel_spmd(nc, in_maps, core_ids=list(range(NB)))
    return np.stack([np.asarray(res.results[b]["out"], np.float32) for b in range(NB)])
